# version 04
# baseline (speedup 1.0000x reference)
import math
import contextlib
import numpy as np
import ml_dtypes
import concourse.bass as bass
import concourse.mybir as mybir
from concourse.bass_utils import run_bass_kernel_spmd

F32 = mybir.dt.float32
BF16 = mybir.dt.bfloat16
AF = mybir.ActivationFunctionType
ALU = mybir.AluOpType
AX = mybir.AxisListType

NCORE = 8
NSEQ = 4
T = 2048
D = 1024
NC = D // 128
NT = T // 128
NHEAD = 16
DH = 64
NF = 256
HID = 4096
NH = HID // 128
EPS_F = 1e-6
LN_EPS = 1e-5
RATIO2 = 1.0 / NF
EPS_NP = 1e-6 / RATIO2
DN = 1.0 / math.sqrt(math.sqrt(DH))
GAMMA_DEN = 1.0 / (HID * D)
ENGINES = ("tensor", "vector", "scalar", "gpsimd", "sync")


class Prog:

    def __init__(self, nc):
        self.nc = nc
        self.q = {e: [] for e in ENGINES}
        self.semh = {}
        self.semv = {}

    def sem(self, name, ctx):
        self.semh[name] = ctx.enter_context(self.nc.semaphore(name))
        self.semv[name] = 0

    def emit(self, eng, fn):
        self.q[eng].append(fn)

    def inc(self, eng, fn, sem, n=1):
        self.semv[sem] += n
        h = self.semh[sem]
        self.q[eng].append(lambda E, fn=fn, h=h, n=n: fn(E).then_inc(h, n))
        return self.semv[sem]

    def mark(self, sem):
        return self.semv[sem]

    def wait(self, eng, sem, val=None):
        v = self.semv[sem] if val is None else val
        if v <= 0:
            return
        h = self.semh[sem]
        self.q[eng].append(lambda E, h=h, v=v: E.wait_ge(h, v))

    def barrier(self, engines=ENGINES, sems=("t", "v", "s", "ds", "dg", "st")):
        vals = [(s, self.semv[s]) for s in sems if self.semv[s] > 0]
        for e in engines:
            for s, v in vals:
                self.wait(e, s, v)

    def replay(self, block):
        for name in ENGINES:
            def make(qn):
                def run(E):
                    for fn in self.q[qn]:
                        fn(E)
                return run
            getattr(block, name)(make(name))


def build_program(stop=None, nseq=NSEQ, dbgmode=False):
    nc = bass.Bass("TRN2", target_bir_lowering=False)

    xq = nc.dram_tensor("xq", [NSEQ, T, D], F32, kind="ExternalInput").ap()
    wa = {k: nc.dram_tensor(f"wa{k}", [D, D], F32, kind="ExternalInput").ap() for k in "qkvo"}
    aproj = nc.dram_tensor("aproj", [DH, NF], F32, kind="ExternalInput").ap()
    wb = {k: nc.dram_tensor(f"wb{k}", [D, D], F32, kind="ExternalInput").ap() for k in "qkvo"}
    biases = nc.dram_tensor("biases", [8, D], F32, kind="ExternalInput").ap()
    wfc = nc.dram_tensor("wfc", [HID, D], F32, kind="ExternalInput").ap()
    wproj = nc.dram_tensor("wproj", [D, HID], F32, kind="ExternalInput").ap()
    lnwb = nc.dram_tensor("lnwb", [2, HID], F32, kind="ExternalInput").ap()
    eye_f = nc.dram_tensor("eye_f", [128, 128], F32, kind="ExternalInput").ap()
    ones_b = nc.dram_tensor("ones_b", [128, 128], BF16, kind="ExternalInput").ap()
    ones_f = nc.dram_tensor("ones_f", [1, 128], F32, kind="ExternalInput").ap()
    triu_b = nc.dram_tensor("triu_b", [128, 128], BF16, kind="ExternalInput").ap()
    out = nc.dram_tensor("out", [NSEQ, T, D], F32, kind="ExternalOutput").ap()

    dbg = {i: nc.dram_tensor(f"dbg{i}", [128, 47492], BF16, kind="ExternalOutput").ap() for i in range(9)}
    dbgf = nc.dram_tensor("dbgf", [128, 64], F32, kind="ExternalOutput").ap()
    swbT = nc.dram_tensor("swbT", [4, D, D], BF16).ap()
    swfcqT = nc.dram_tensor("swfcqT", [NH, NC, 128, 128], BF16).ap()
    swpT = nc.dram_tensor("swpT", [4, HID, 256], BF16).ap()

    ctx = contextlib.ExitStack()
    with ctx:
        sbt = lambda *a, **k: ctx.enter_context(nc.sbuf_tensor(*a, **k))
        pst = lambda *a, **k: ctx.enter_context(nc.psum_tensor(*a, **k))

        XT = sbt("XT", [128, NC, T], BF16, is_transpose_dest=True)
        Y = sbt("Y", [128, NC, T], BF16, is_transpose_dest=True)
        AR = sbt("AR", [128, 47492], BF16)
        WS = sbt("WS", [128, 2, 8, 256], BF16)
        RB = sbt("RB", [128, T], BF16)
        MRB = sbt("MRB", [128, T], BF16)
        SR = sbt("SR", [128, 2048], F32)
        SRB = sbt("SRB", [128, 2048], BF16)
        CONS = sbt("CONS", [128, 2, 128], BF16)
        EYEF = sbt("EYEF", [128, 128], F32)
        ONEF = sbt("ONEF", [1, 128], F32)
        BIAST = sbt("BIAST", [128, 8, 8], F32)
        LNC = sbt("LNC", [128, NH, 2], BF16)
        LNCF = sbt("LNCF", [128, NH], F32)
        PAUG = sbt("PAUG", [65, NF], BF16)
        S2C = sbt("S2C", [128, NC], F32)
        S1R = sbt("S1R", [1, D], F32)
        RBF = sbt("RBF", [128, 512], F32)
        FFS = sbt("FFS", [128, 1024], F32)
        THR = sbt("THR", [128, 4], F32)
        ACC = sbt("ACC", [128, 2], F32)
        NEGB = sbt("NEGB", [128, 2], F32)
        BVB = sbt("BVB", [128, D], BF16)
        CB = sbt("CB", [128, 2], F32)

        onesb = CONS[:, 0, :]
        triub = CONS[:, 1, :]
        ARF = AR[:, :]

        def arv(off, shape):
            n = int(np.prod(shape[1:]))
            ap = ARF[:, off:off + n]
            if len(shape) == 3:
                ap = ap.rearrange("p (a b) -> p a b", a=shape[1])
            elif len(shape) == 4:
                ap = ap.rearrange("p (a b c) -> p a b c", a=shape[1], b=shape[2])
            return ap

        VAUG = arv(0, [128, NT, NHEAD * 65])
        OHT = arv(16640, [128, NC, T])
        QP = arv(33024, [128, T])
        KP = arv(35072, [128, T])
        QAK = arv(37120, [128, T])
        KPR = arv(39168, [128, NT, NF])
        QPR = arv(43264, [128, 2, T])
        KVA = arv(47360, [128, 2, 65])
        NUM = arv(37120, [128, T])
        QSQ = arv(39168, [128, T])
        PBUF = arv(39168, [128, 2, 512])
        XINV = arv(16640, [128, NT, D])
        WVB = arv(16640, [128, NC, D])
        HB = arv(0, [128, NH, 512])
        WQS = arv(16640, [128, 2, NH, 256])
        FSQ = arv(33024, [128, 2, 512])
        FT1 = arv(34048, [128, 512])
        WFS = arv(35072, [128, 2, 8, 128])
        XSQ = arv(0, [128, 2, 512])
        PATMP = arv(0, [128, 128])
        PSTG = arv(128, [128, 2, 128])
        PQS = arv(384, [128, 2, 128])

        PA = pst("PA", [128, 2048], F32)
        PB = pst("PB", [128, 2048], F32)

        prog = Prog(nc)
        for s_ in ("t", "v", "s", "ds", "dg", "st"):
            prog.sem(s_, ctx)
        P = prog

        def vfence():
            P.wait("vector", "v")

        def sfence():
            P.wait("scalar", "s")

        def dma(q, dst, src, sem):
            return P.inc(q, lambda E, d=dst, s=src: E.dma_start(out=d, in_=s), sem, 16)

        def dump(i, buf):
            if not dbgmode:
                return
            P.barrier()
            n = int(np.prod(buf.shape[1:]))
            dma("gpsimd", dbg[i][:, 0:n], buf.rearrange("p a b -> p (a b)") if len(buf.shape) == 3 else buf, "st")
            P.barrier()

        def dmat(dst, src):
            return P.inc("sync", lambda E, d=dst, s=src: E.dma_start_transpose(d, s), "ds", 16)

        def mm(o, l, r, st_, sp):
            return lambda E, o=o, l=l, r=r, a=st_, b=sp: E.matmul(o, l, r, start=a, stop=b)

        def act(o, i, f, bias=None, scale=None):
            kw = {}
            if bias is not None:
                kw["bias"] = bias
            if scale is not None:
                kw["scale"] = scale
            return lambda E, o=o, i=i, f=f, kw=kw: E.activation(o, i, f, **kw)

        def tp(o, i, np_=128):
            return lambda E, o=o, i=i, np_=np_: E.transpose(o, i, EYEF[0:np_, 0:np_])

        P.inc("vector", lambda E: E.memset(CB[:, 0:1], LN_EPS), "v")
        P.inc("vector", lambda E: E.memset(CB[:, 1:2], 0.5e-6), "v")
        dma("sync", CONS[:, 0, :], ones_b, "ds")
        dma("sync", CONS[:, 1, :], triu_b, "ds")
        dma("sync", EYEF[:, :], eye_f, "ds")
        dma("sync", ONEF[:, :], ones_f, "ds")
        dma("gpsimd", SR[0:8, 0:1024], biases, "dg")
        P.wait("tensor", "ds")
        P.wait("tensor", "dg")
        ev_marks = {}
        for oc in range(8):
            slot = (oc % 2) * 512
            if oc >= 2:
                P.wait("tensor", "s", ev_marks[oc - 2])
            P.inc("tensor", tp(PA[:, slot:slot + 8], SR[0:8, oc * 128:oc * 128 + 128], 8), "t")
            P.wait("scalar", "t")
            ev_marks[oc] = P.inc("scalar", act(BIAST[:, oc, :], PA[:, slot:slot + 8], AF.Copy), "s")
        for half in range(2):
            P.wait("sync", "s")
            dma("sync", SR[0:2, 0:2048], lnwb[:, half * 2048:half * 2048 + 2048], "ds")
            P.wait("tensor", "ds")
            ev_marks = {}
            for j in range(16):
                ht = half * 16 + j
                slot = (j % 2) * 512
                if j >= 2:
                    P.wait("tensor", "s", ev_marks[j - 2])
                P.inc("tensor", tp(PA[:, slot:slot + 2], SR[0:2, j * 128:j * 128 + 128], 2), "t")
                P.wait("scalar", "t")
                P.inc("scalar", act(LNC[:, ht, :], PA[:, slot:slot + 2], AF.Copy), "s")
                ev_marks[j] = P.inc("scalar", act(LNCF[:, ht:ht + 1], PA[:, slot:slot + 1], AF.Copy), "s")
        P.wait("gpsimd", "s")
        dma("gpsimd", SR[0:64, 0:256], aproj, "dg")
        P.wait("vector", "dg")
        P.inc("vector", lambda E: E.tensor_scalar_mul(PAUG[0:64, :], SR[0:64, 0:256], DN), "v")
        P.inc("vector", lambda E: E.memset(PAUG[64:65, :], -1.0), "v")
        P.barrier()

        for i, k in enumerate("qkvo"):
            blk_ev = {}
            blk_st = {}
            ld_done = {}
            for ot in range(8):
                sl = (ot % 2) * 1024
                if ot >= 2:
                    P.wait("gpsimd", "t", ld_done[ot - 2])
                dma("gpsimd", SR[:, sl:sl + 1024], wb[k][ot * 128:ot * 128 + 128, :], "dg")
                P.wait("tensor", "dg")
                for ct in range(8):
                    bi = ct % 2
                    idx = ot * 8 + ct
                    if idx >= 2:
                        P.wait("tensor", "s", blk_ev[(idx - 2) % 4])
                    P.inc("tensor", tp(PA[:, bi * 512:bi * 512 + 128],
                                       SR[:, sl + ct * 128:sl + ct * 128 + 128]), "t")
                    P.wait("scalar", "t")
                    if idx >= 2:
                        P.wait("scalar", "ds", blk_st[(idx - 2) % 4])
                    blk_ev[idx % 4] = P.inc("scalar", act(PSTG[:, bi, :], PA[:, bi * 512:bi * 512 + 128],
                                                          AF.Copy), "s")
                    P.wait("sync", "s", blk_ev[idx % 4])
                    blk_st[idx % 4] = dma("sync", swbT[i, ct * 128:ct * 128 + 128, ot * 128:ot * 128 + 128],
                                          PSTG[:, bi, :], "ds")
                ld_done[ot] = P.mark("t")
            P.barrier()

        P.inc("vector", lambda E: E.memset(ACC[:, 0:1], 0.0), "v")
        ld_done = {}
        for it in range(NH):
            sl = (it % 2) * 1024
            if it >= 2:
                P.wait("gpsimd", "v", ld_done[it - 2])
            dma("gpsimd", SR[:, sl:sl + 1024], wfc[it * 128:it * 128 + 128, :], "dg")
            P.wait("vector", "dg")
            vfence()
            P.inc("vector", lambda E, sl=sl: E.tensor_reduce(ACC[:, 1:2], SR[:, sl:sl + 1024], AX.X, ALU.add,
                                                             apply_absolute_value=True), "v")
            vfence()
            ld_done[it] = P.inc("vector", lambda E: E.tensor_add(ACC[:, 0:1], ACC[:, 0:1], ACC[:, 1:2]), "v")

        def emit_thr(dbgcol=None):
            if dbgmode and dbgcol is not None:
                P.barrier()
                dma("gpsimd", dbgf[:, dbgcol:dbgcol + 2], ACC[:, :], "st")
                P.barrier()
            P.wait("tensor", "v")
            P.inc("tensor", tp(PA[0:1, 0:128], ACC[:, 0:1]), "t")
            P.wait("vector", "t")
            P.inc("vector", lambda E: E.tensor_reduce(SR[0:1, 0:1], PA[0:1, 0:128], AX.X, ALU.add), "v")
            vfence()
            P.inc("vector", lambda E: E.tensor_scalar(SR[0:1, 1:2], SR[0:1, 0:1], 0.5 * GAMMA_DEN, 0.5e-6,
                                                      ALU.mult, ALU.add), "v")
            P.wait("tensor", "v")
            P.inc("tensor", mm(PA[:, 0:1], ONEF[0:1, :], SR[0:1, 1:2], True, True), "t")
            P.wait("scalar", "t")
            P.inc("scalar", act(THR[:, 0:1], PA[:, 0:1], AF.Copy), "s")
            P.inc("scalar", act(THR[:, 1:2], PA[:, 0:1], AF.Copy, scale=-1.0), "s")
            if dbgmode and dbgcol is not None:
                P.barrier()
                dma("gpsimd", dbgf[0:1, dbgcol + 2:dbgcol + 6], SR[0:1, 0:4], "st")
                dma("gpsimd", dbgf[:, dbgcol + 6:dbgcol + 8], THR[:, 0:2], "st")
                P.barrier()
            P.barrier()
        emit_thr(dbgcol=16)

        def quant_ops(src, dst, lnw_ap=None, unscaled=None):
            vfence()
            P.inc("vector", lambda E, s=src: E.tensor_single_scalar(PATMP[:, :], s, THR[:, 1:2], ALU.is_le), "v")
            vfence()
            if lnw_ap is None:
                return P.inc("vector", lambda E, s=src, d=dst:
                             E.scalar_tensor_tensor(d, s, THR[:, 0:1], PATMP[:, :], ALU.is_ge, ALU.subtract), "v")
            P.inc("vector", lambda E, s=src, d=unscaled:
                  E.scalar_tensor_tensor(d, s, THR[:, 0:1], PATMP[:, :], ALU.is_ge, ALU.subtract), "v")
            vfence()
            return P.inc("vector", lambda E, d=dst, e=unscaled, l=lnw_ap:
                         E.tensor_single_scalar(d, e, l, ALU.mult), "v")

        blk_q = {}
        blk_st = {}
        ld_done = {}
        for it in range(NH):
            sl = (it % 2) * 1024
            if it >= 2:
                P.wait("gpsimd", "t", ld_done[it - 2])
            dma("gpsimd", SR[:, sl:sl + 1024], wfc[it * 128:it * 128 + 128, :], "dg")
            P.wait("tensor", "dg")
            for ct in range(8):
                bi = ct % 2
                idx = it * 8 + ct
                if idx >= 2:
                    P.wait("tensor", "v", blk_q[(idx - 2) % 4])
                P.inc("tensor", tp(PA[:, bi * 512:bi * 512 + 128],
                                   SR[:, sl + ct * 128:sl + ct * 128 + 128]), "t")
                P.wait("vector", "t")
                if idx >= 2:
                    P.wait("vector", "ds", blk_st[(idx - 2) % 4])
                blk_q[idx % 4] = quant_ops(PA[:, bi * 512:bi * 512 + 128], PSTG[:, bi, :])
                P.wait("sync", "v", blk_q[idx % 4])
                blk_st[idx % 4] = dma("sync", swfcqT[it, ct], PSTG[:, bi, :], "ds")
            ld_done[it] = P.mark("t")
        P.barrier()

        P.inc("vector", lambda E: E.memset(ACC[:, 0:1], 0.0), "v")
        ld_done = {}
        for it in range(32):
            dt, ch = it // 4, it % 4
            sl = (it % 2) * 1024
            if it >= 2:
                P.wait("gpsimd", "v", ld_done[it - 2])
            dma("gpsimd", SR[:, sl:sl + 1024], wproj[dt * 128:dt * 128 + 128, ch * 1024:ch * 1024 + 1024], "dg")
            P.wait("vector", "dg")
            vfence()
            P.inc("vector", lambda E, sl=sl: E.tensor_reduce(ACC[:, 1:2], SR[:, sl:sl + 1024], AX.X, ALU.add,
                                                             apply_absolute_value=True), "v")
            vfence()
            ld_done[it] = P.inc("vector", lambda E: E.tensor_add(ACC[:, 0:1], ACC[:, 0:1], ACC[:, 1:2]), "v")
        emit_thr(dbgcol=24)

        for dt in range(8):
            for rnd in range(2):
                P.wait("gpsimd", "t")
                P.wait("gpsimd", "v")
                dma("gpsimd", SR[:, 0:1024], wproj[dt * 128:dt * 128 + 128, rnd * 2048:rnd * 2048 + 1024], "dg")
                dma("gpsimd", SR[:, 1024:2048],
                    wproj[dt * 128:dt * 128 + 128, rnd * 2048 + 1024:rnd * 2048 + 2048], "dg")
                P.wait("tensor", "dg")
                blk_q = {}
                blk_st = {}
                blk_mm = {}
                for j in range(16):
                    ht = rnd * 16 + j
                    bi = j % 2
                    if j >= 2:
                        P.wait("tensor", "v", blk_q[(j - 2) % 4])
                    P.inc("tensor", tp(PA[:, bi * 512:bi * 512 + 128], SR[:, j * 128:j * 128 + 128]), "t")
                    P.wait("vector", "t")
                    if j >= 2:
                        P.wait("vector", "ds", blk_st[(j - 2) % 4])
                        P.wait("vector", "t", blk_mm[(j - 2) % 4])
                    blk_q[j % 4] = quant_ops(PA[:, bi * 512:bi * 512 + 128], PSTG[:, bi, :],
                                             lnw_ap=LNCF[:, ht:ht + 1], unscaled=PQS[:, bi, :])
                    P.wait("tensor", "v", blk_q[j % 4])
                    blk_mm[j % 4] = P.inc("tensor", mm(PB[:, 0:2], PQS[:, bi, :], LNC[:, ht, :],
                                                       ht == 0, ht == NH - 1), "t")
                    P.wait("sync", "v", blk_q[j % 4])
                    blk_st[j % 4] = dma("sync",
                                        swpT[dt // 2, ht * 128:ht * 128 + 128, (dt % 2) * 128:(dt % 2) * 128 + 128],
                                        PSTG[:, bi, :], "ds")
            P.wait("scalar", "t")
            P.inc("scalar", act(S2C[:, dt:dt + 1], PB[:, 1:2], AF.Copy), "s")
            P.inc("scalar", act(THR[:, 2:3], PB[:, 0:1], AF.Copy), "s")
            P.wait("tensor", "s")
            P.inc("tensor", tp(PA[0:1, 1024:1152], THR[:, 2:3]), "t")
            P.wait("scalar", "t")
            P.inc("scalar", act(S1R[0:1, dt * 128:dt * 128 + 128], PA[0:1, 1024:1152], AF.Copy, scale=-1.0), "s")
            P.barrier(("tensor",), sems=("s",))
        P.barrier()

        if dbgmode:
            P.barrier()
            dma("gpsimd", dbgf[:, 8:12], THR[:, :], "st")
            for _i in range(NH):
                dma("gpsimd", dbg[3][:, _i * 1024:(_i + 1) * 1024].rearrange("p (b j) -> p b j", b=NC),
                    swfcqT[_i].rearrange("b p j -> p b j"), "st")
            P.barrier()
        if stop == "preproc":
            nseq = 0

        def emit_entry(s):
            dma("gpsimd", XINV[:, :, :], xq[s].rearrange("(tt p) d -> p tt d", p=128), "dg")
            P.wait("sync", "dg")
            for tt in range(NT):
                dmat(XT[:, :, tt * 128:tt * 128 + 128], XINV[:, tt, :])
            P.barrier()

        def emit_preln():
            sq_used = {}
            for tc in range(4):
                tsl = slice(tc * 512, tc * 512 + 512)
                for c in range(NC):
                    bi = c % 2
                    idx = tc * NC + c
                    if idx >= 2:
                        P.wait("vector", "t", sq_used[(idx - 2) % 4])
                    v = P.inc("vector", lambda E, c=c, bi=bi, tsl=tsl:
                              E.tensor_mul(XSQ[:, bi, :], XT[:, c, tsl], XT[:, c, tsl]), "v")
                    P.inc("tensor", mm(PB[0:1, tsl], onesb[:, 0:1], XT[:, c, tsl], c == 0, c == NC - 1), "t")
                    P.wait("tensor", "v", v)
                    sq_used[idx % 4] = P.inc("tensor", mm(PA[0:1, tsl], onesb[:, 0:1], XSQ[:, bi, :],
                                                          c == 0, c == NC - 1), "t")
            P.wait("scalar", "t")
            for tc in range(4):
                a = tc * 512
                M_, Q_, T_, R_ = SR[0:1, 0:512], SR[0:1, 512:1024], SR[0:1, 1024:1536], SR[0:1, 1536:2048]
                P.inc("scalar", act(M_, PB[0:1, a:a + 512], AF.Copy, scale=1.0 / D), "s")
                P.inc("scalar", act(Q_, PA[0:1, a:a + 512], AF.Copy, scale=1.0 / D), "s")
                P.wait("vector", "s")
                P.inc("vector", lambda E, M_=M_, T_=T_: E.tensor_mul(T_, M_, M_), "v")
                vfence()
                P.inc("vector", lambda E, Q_=Q_, T_=T_: E.tensor_sub(T_, Q_, T_), "v")
                P.wait("scalar", "v")
                P.inc("scalar", act(R_, T_, AF.Ln, bias=CB[0:1, 0:1]), "s")
                sfence()
                P.inc("scalar", act(R_, R_, AF.Exp, scale=-0.5), "s")
                P.wait("vector", "s")
                P.inc("vector", lambda E, M_=M_, R_=R_: E.tensor_mul(M_, M_, R_), "v")
                P.wait("scalar", "v")
                P.inc("scalar", act(SRB[0:1, a:a + 512], R_, AF.Copy), "s")
                sm_ = P.inc("scalar", act(RB[0:1, a:a + 512], M_, AF.Copy), "s")
                if tc < 3:
                    P.wait("scalar", "s", sm_)
            P.wait("tensor", "s")
            for j in range(4):
                jsl = slice(j * 512, j * 512 + 512)
                P.inc("tensor", mm(PA[:, jsl], onesb[0:1, :], SRB[0:1, jsl], True, True), "t")
                P.inc("tensor", mm(PB[:, jsl], onesb[0:1, :], RB[0:1, jsl], True, True), "t")
            P.wait("scalar", "t")
            P.inc("scalar", act(RB[:, :], PA[:, :], AF.Copy), "s")
            P.inc("scalar", act(MRB[:, :], PB[:, :], AF.Copy), "s")
            P.wait("vector", "s")
            for c in range(NC):
                P.inc("vector", lambda E, c=c: E.tensor_mul(Y[:, c, :], XT[:, c, :], RB[:, :]), "v")
                vfence()
                P.inc("vector", lambda E, c=c: E.tensor_sub(Y[:, c, :], Y[:, c, :], MRB[:, :]), "v")
            P.barrier()

        def emit_bvb(bias_idx):
            dma("gpsimd", SRB[0:1, 0:1024], biases[bias_idx:bias_idx + 1, :], "dg")
            P.wait("tensor", "dg")
            P.inc("tensor", mm(PA[:, 0:512], onesb[0:1, :], SRB[0:1, 0:512], True, True), "t")
            P.inc("tensor", mm(PA[:, 512:1024], onesb[0:1, :], SRB[0:1, 512:1024], True, True), "t")
            P.wait("scalar", "t")
            P.inc("scalar", act(BVB[:, :], PA[:, 0:1024], AF.Copy), "s")
            P.barrier()

        def emit_vproj(wmark):
            P.wait("tensor", wmark[0], wmark[1])
            ev = {}
            for tt in range(NT):
                tsl = slice(tt * 128, tt * 128 + 128)
                reg = PA if tt % 2 == 0 else PB
                if tt >= 2:
                    P.wait("tensor", "v", ev[tt - 2])
                for c in range(NC):
                    P.inc("tensor", mm(reg[:, 0:512], Y[:, c, tsl], WVB[:, c, 0:512], c == 0, c == NC - 1), "t")
                    P.inc("tensor", mm(reg[:, 512:1024], Y[:, c, tsl], WVB[:, c, 512:1024], c == 0, c == NC - 1), "t")
                P.wait("vector", "t")
                ev[tt] = P.inc("vector", lambda E, tt=tt, reg=reg:
                               E.tensor_add(VAUG[:, tt, :].rearrange("p (h e) -> p h e", h=NHEAD)[:, :, 0:64],
                                            reg[:, 0:1024].rearrange("p (h e) -> p h e", h=NHEAD),
                                            BVB[:, :].rearrange("p (h e) -> p h e", h=NHEAD)), "v")
            for tt in range(NT):
                P.inc("vector", lambda E, tt=tt:
                      E.memset(VAUG[:, tt, :].rearrange("p (h e) -> p h e", h=NHEAD)[:, :, 64:65], 1.0), "v")
            P.barrier()

        def load_wslice(wsrc, o, buf, colsl, cast):
            src = wsrc[:, o * 128:o * 128 + 128].rearrange("(c p) o -> p c o", p=128)
            if cast:
                dma("gpsimd", WS[:, buf, :, colsl], src, "dg")
                return ("dg", P.mark("dg"))
            dma("sync", WS[:, buf, :, colsl], src, "ds")
            return ("ds", P.mark("ds"))

        def emit_pair_qk(mq, mk, bq_ap, bk_ap):
            P.wait("tensor", mq[0], mq[1])
            for c in range(NC):
                for j in range(4):
                    jsl = slice(j * 512, j * 512 + 512)
                    P.inc("tensor", mm(PA[:, jsl], WS[:, 0, c, 0:128], Y[:, c, jsl], c == 0, c == NC - 1), "t")
            P.wait("vector", "t")
            P.inc("vector", lambda E, b=bq_ap: E.tensor_single_scalar(QP[:, :], PA[:, :], b, ALU.add), "v")
            P.wait("tensor", mk[0], mk[1])
            for c in range(NC):
                for j in range(4):
                    jsl = slice(j * 512, j * 512 + 512)
                    P.inc("tensor", mm(PB[:, jsl], WS[:, 0, c, 128:256], Y[:, c, jsl], c == 0, c == NC - 1), "t")
            P.wait("vector", "t")
            P.inc("vector", lambda E, b=bk_ap: E.tensor_single_scalar(KP[:, :], PB[:, :], b, ALU.add), "v")
            P.barrier(("tensor", "vector", "scalar"), sems=("s", "v"))

        def emit_perf_head(p, hh):
            hs = hh * 64
            h = 2 * p + hh
            P.inc("vector", lambda E: E.tensor_copy(QAK[0:64, :], QP[hs:hs + 64, :]), "v")
            vv = P.inc("vector", lambda E: E.tensor_mul(QSQ[hs:hs + 64, :], QP[hs:hs + 64, :], QP[hs:hs + 64, :]), "v")
            P.wait("tensor", "v", vv)
            for j in range(4):
                jsl = slice(j * 512, j * 512 + 512)
                P.inc("tensor", mm(PA[0:1, jsl], onesb[hs:hs + 64, 0:1], QSQ[hs:hs + 64, jsl], True, True), "t")
            P.wait("scalar", "t")
            P.inc("scalar", act(QAK[64:65, :], PA[0:1, :], AF.Copy, scale=DN * DN * 0.5), "s")
            P.wait("tensor", "s")
            for fc in range(2):
                reg = PA if fc == 0 else PB
                for j in range(4):
                    jsl = slice(j * 512, j * 512 + 512)
                    P.inc("tensor", mm(reg[:, jsl], PAUG[:, fc * 128:fc * 128 + 128], QAK[0:65, jsl], True, True), "t")
            P.wait("scalar", "t")
            for fc in range(2):
                reg = PA if fc == 0 else PB
                P.inc("scalar", act(QPR[:, fc, :], reg[:, :], AF.Exp), "s")
            P.wait("vector", "s")
            P.inc("vector", lambda E: E.tensor_scalar_add(QPR[:, :, :], QPR[:, :, :], EPS_F), "v")
            P.inc("vector", lambda E: E.tensor_copy(QAK[0:64, :], KP[hs:hs + 64, :]), "v")
            vv = P.inc("vector", lambda E: E.tensor_mul(QSQ[hs:hs + 64, :], KP[hs:hs + 64, :], KP[hs:hs + 64, :]), "v")
            P.wait("tensor", "v", vv)
            for j in range(4):
                jsl = slice(j * 512, j * 512 + 512)
                P.inc("tensor", mm(PA[0:1, jsl], onesb[hs:hs + 64, 0:1], QSQ[hs:hs + 64, jsl], True, True), "t")
            P.wait("scalar", "t")
            P.inc("scalar", act(QAK[64:65, :], PA[0:1, :], AF.Copy, scale=DN * DN * 0.5), "s")
            P.wait("tensor", "s")
            kev = {}
            for tt in range(NT):
                slot = (tt % 4) * 512
                if tt >= 4:
                    P.wait("tensor", "s", kev[tt - 4])
                P.inc("tensor", mm(PA[:, slot:slot + 256], QAK[0:65, tt * 128:tt * 128 + 128], PAUG[:, :],
                                   True, True), "t")
                P.wait("scalar", "t")
                kev[tt] = P.inc("scalar", act(KPR[:, tt, :], PA[:, slot:slot + 256], AF.Exp), "s")
            P.wait("vector", "s")
            vv = P.inc("vector", lambda E: E.tensor_scalar_add(KPR[:, :, :], KPR[:, :, :], EPS_F), "v")
            P.wait("tensor", "v", vv)
            for tt in range(NT):
                P.inc("tensor", mm(PB[0:65, 0:256], VAUG[:, tt, h * 65:h * 65 + 65], KPR[:, tt, :],
                                   tt == 0, tt == NT - 1), "t")
            P.wait("scalar", "t")
            P.inc("scalar", act(SR[0:65, 0:256], PB[0:65, 0:256], AF.Copy), "s")
            P.wait("tensor", "s")
            for fc in range(2):
                P.inc("tensor", tp(PB[:, 1024 + fc * 256:1024 + fc * 256 + 65],
                                   SR[0:65, fc * 128:fc * 128 + 128], 65), "t")
            P.wait("scalar", "t")
            for fc in range(2):
                P.inc("scalar", act(KVA[:, fc, :], PB[:, 1024 + fc * 256:1024 + fc * 256 + 65], AF.Copy), "s")
            P.wait("tensor", "s")
            for fc in range(2):
                for j in range(4):
                    jsl = slice(j * 512, j * 512 + 512)
                    P.inc("tensor", mm(PA[0:65, jsl], KVA[:, fc, :], QPR[:, fc, jsl], fc == 0, fc == 1), "t")
            P.wait("vector", "t")
            P.inc("vector", lambda E: E.tensor_single_scalar(SR[0:1, :], PA[64:65, :], EPS_NP, ALU.add), "v")
            vfence()
            P.inc("vector", lambda E: E.reciprocal(SR[32:33, :], SR[0:1, :]), "v")
            P.wait("scalar", "v")
            cm = P.inc("scalar", act(SRB[0:1, :], SR[32:33, :], AF.Copy), "s")
            sm = P.inc("scalar", act(NUM[0:64, :], PA[0:64, :], AF.Copy), "s")
            P.wait("tensor", "s", cm)
            for j in range(4):
                jsl = slice(j * 512, j * 512 + 512)
                P.inc("tensor", mm(PB[:, jsl], onesb[0:1, :], SRB[0:1, jsl], True, True), "t")
            P.wait("vector", "t")
            P.wait("vector", "s", sm)
            P.inc("vector", lambda E: E.tensor_mul(OHT[hs:hs + 64, p, :], NUM[0:64, :], PB[0:64, :]), "v")
            P.barrier(("tensor", "scalar"), sems=("v",))

        def emit_out_proj(wsrc, bias_which, cast):
            marks = {0: load_wslice(wsrc, 0, 0, slice(0, 128), cast),
                     1: load_wslice(wsrc, 1, 1, slice(0, 128), cast)}
            for o in range(NC):
                reg = PA if o % 2 == 0 else PB
                P.wait("tensor", marks[o][0], marks[o][1])
                for c in range(NC):
                    for j in range(4):
                        jsl = slice(j * 512, j * 512 + 512)
                        P.inc("tensor", mm(reg[:, jsl], WS[:, o % 2, c, 0:128], OHT[:, c, jsl],
                                           c == 0, c == NC - 1), "t")
                tmark = P.mark("t")
                if o + 2 < NC:
                    P.wait("gpsimd" if cast else "sync", "t", tmark)
                    marks[o + 2] = load_wslice(wsrc, o + 2, o % 2, slice(0, 128), cast)
                P.wait("vector", "t", tmark)
                P.inc("vector", lambda E, o=o, reg=reg:
                      E.scalar_tensor_tensor(XT[:, o, :], reg[:, :], BIAST[:, o, bias_which:bias_which + 1], XT[:, o, :],
                                             ALU.add, ALU.add), "v")
            P.barrier()

        def emit_attn_a(s):
            emit_preln()
            if s == 0:
                dump(0, Y[:, :, :])
            emit_bvb(2)
            dma("gpsimd", WVB[:, :, :], wa['v'].rearrange("(c p) o -> p c o", p=128), "dg")
            emit_vproj(("dg", P.mark("dg")))
            if s == 0:
                dump(1, ARF)
            for p in range(8):
                mq = load_wslice(wa['q'], p, 0, slice(0, 128), True)
                mk = load_wslice(wa['k'], p, 0, slice(128, 256), True)
                emit_pair_qk(mq, mk, BIAST[:, p, 0:1], BIAST[:, p, 1:2])
                if s == 0 and p == 0:
                    dump(2, ARF)
                emit_perf_head(p, 0)
                if s == 0 and p == 0:
                    dump(3, ARF)
                emit_perf_head(p, 1)
                P.barrier()
            if s == 0:
                dump(4, ARF)
            emit_out_proj(wa['o'], 3, True)
            if s == 0:
                dump(5, XT[:, :, :])

        def emit_ffn(s, dbase=None, final=False):
            emit_preln()
            if dbase is not None:
                dump(dbase, Y[:, :, :])
            for tc in range(4):
                tsl = slice(tc * 512, tc * 512 + 512)
                wf_marks = {0: dma("sync", WFS[:, 0, :, :], swfcqT[0].rearrange("c p j -> p c j"), "ds")}
                hb_done = {}
                for ht in range(NH):
                    bi = ht % 2
                    P.wait("tensor", "ds", wf_marks[ht])
                    if ht >= 2:
                        P.wait("tensor", "s", hb_done[ht - 2])
                    for c in range(NC):
                        P.inc("tensor", mm(PB[:, bi * 512:bi * 512 + 512], WFS[:, bi, c, :], Y[:, c, tsl],
                                           c == 0, c == NC - 1), "t")
                    tmark = P.mark("t")
                    if ht + 1 < NH:
                        if ht >= 1:
                            P.wait("sync", "t", tmark)
                        wf_marks[ht + 1] = dma("sync", WFS[:, (ht + 1) % 2, :, :],
                                               swfcqT[ht + 1].rearrange("c p j -> p c j"), "ds")
                    P.wait("scalar", "t", tmark)
                    hb_done[ht] = P.inc("scalar", act(HB[:, ht, :], PB[:, bi * 512:bi * 512 + 512], AF.Gelu), "s")
                    P.wait("vector", "s", hb_done[ht])
                    if ht >= 2:
                        P.wait("vector", "t", sqmm[ht - 2])
                    sv = P.inc("vector", lambda E, ht=ht, bi=bi:
                               E.tensor_mul(FSQ[:, bi, :], HB[:, ht, :], HB[:, ht, :]), "v")
                    P.wait("tensor", "s", hb_done[ht])
                    P.inc("tensor", mm(PB[0:1, 1024:1536], onesb[:, 0:1], HB[:, ht, :], ht == 0, ht == NH - 1), "t")
                    P.wait("tensor", "v", sv)
                    P.inc("tensor", mm(PB[0:1, 1536:2048], onesb[:, 0:1], FSQ[:, bi, :], ht == 0, ht == NH - 1), "t")
                    if ht == 0:
                        sqmm = {}
                    sqmm[ht] = P.mark("t")
                P.wait("scalar", "t")
                M_, Q_, T_, R_ = SR[0:1, 0:512], SR[0:1, 512:1024], SR[0:1, 1024:1536], SR[0:1, 1536:2048]
                P.inc("scalar", act(M_, PB[0:1, 1024:1536], AF.Copy, scale=1.0 / HID), "s")
                P.inc("scalar", act(Q_, PB[0:1, 1536:2048], AF.Copy, scale=1.0 / HID), "s")
                P.wait("vector", "s")
                P.inc("vector", lambda E: E.tensor_mul(T_, M_, M_), "v")
                vfence()
                P.inc("vector", lambda E: E.tensor_sub(T_, Q_, T_), "v")
                P.wait("scalar", "v")
                P.inc("scalar", act(R_, T_, AF.Ln, bias=CB[0:1, 0:1]), "s")
                sfence()
                P.inc("scalar", act(R_, R_, AF.Exp, scale=-0.5), "s")
                P.wait("tensor", "s")
                P.inc("tensor", mm(PA[:, 1536:2048], ONEF[0:1, :], R_, True, True), "t")
                P.wait("scalar", "t")
                rbm = P.inc("scalar", act(RBF[:, :], PA[:, 1536:2048], AF.Copy), "s")
                if dbase is not None and tc == 0:
                    dump(dbase + 1, ARF)
                    P.barrier()
                    dma("gpsimd", dbgf[:, 0:8], S2C[:, :], "st")
                    dma("gpsimd", dbg[dbase + 2][0:1, 0:1024], S1R[0:1, :], "st")
                    dma("gpsimd", dbg[dbase + 2][0:1, 1024:1536], SRB[0:1, 0:512], "st")
                    dma("gpsimd", dbg[dbase + 2][:, 2048:2560], RB[:, 0:512], "st")
                    P.barrier()
                wq_marks = {0: dma("sync", WQS[:, 0, :, :], swpT[0].rearrange("(h p) e -> p h e", p=128), "ds")}
                g_ev = {}
                for qd in range(4):
                    bi = qd % 2
                    P.wait("tensor", "ds", wq_marks[qd])
                    P.wait("tensor", "s", rbm)
                    if qd >= 2:
                        P.wait("tensor", "v", g_ev[qd - 2])
                    for dc in range(2):
                        psl = slice(bi * 1024 + dc * 512, bi * 1024 + dc * 512 + 512)
                        for ht in range(NH):
                            P.inc("tensor", mm(PA[:, psl], WQS[:, bi, ht, dc * 128:dc * 128 + 128], HB[:, ht, :],
                                               ht == 0, False), "t")
                        P.inc("tensor", mm(PA[:, psl], S1R[0:1, qd * 256 + dc * 128:qd * 256 + dc * 128 + 128],
                                           M_, False, True), "t")
                    tmark = P.mark("t")
                    if qd + 1 < 4:
                        if qd >= 1:
                            P.wait("sync", "t", tmark)
                        wq_marks[qd + 1] = dma("sync", WQS[:, (qd + 1) % 2, :, :],
                                               swpT[qd + 1].rearrange("(h p) e -> p h e", p=128), "ds")
                    P.wait("vector", "t", tmark)
                    for dc in range(2):
                        d = qd * 2 + dc
                        psl = slice(bi * 1024 + dc * 512, bi * 1024 + dc * 512 + 512)
                        FT1F = FFS[:, 0:512]
                        OF = FFS[:, 512:1024]
                        if not final:
                            vfence()
                            P.inc("vector", lambda E, psl=psl, FT1F=FT1F:
                                  E.tensor_mul(FT1F, PA[:, psl], RBF[:, :]), "v")
                            vfence()
                            g_ev[qd] = P.inc("vector", lambda E, d=d, tsl=tsl, FT1F=FT1F:
                                             E.scalar_tensor_tensor(XT[:, d, tsl], FT1F, S2C[:, d:d + 1],
                                                                    XT[:, d, tsl], ALU.add, ALU.add), "v")
                        else:
                            P.wait("vector", "t")
                            vfence()
                            P.inc("vector", lambda E, psl=psl, FT1F=FT1F:
                                  E.tensor_mul(FT1F, PA[:, psl], RBF[:, :]), "v")
                            vfence()
                            P.inc("vector", lambda E, d=d, tsl=tsl, FT1F=FT1F, OF=OF:
                                  E.scalar_tensor_tensor(OF, FT1F, S2C[:, d:d + 1], XT[:, d, tsl],
                                                         ALU.add, ALU.add), "v")
                            mvv = P.mark("v")
                            for sub in range(4):
                                tb = sub % 2
                                P.wait("tensor", "v", mvv)
                                if True:
                                    P.inc("tensor", tp(PB[:, tb * 512:tb * 512 + 128],
                                                       OF[:, sub * 128:sub * 128 + 128]), "t")
                                P.wait("scalar", "t")
                                smv = P.inc("scalar", act(SR[:, 1024 + tb * 128:1024 + tb * 128 + 128],
                                                          PB[:, tb * 512:tb * 512 + 128], AF.Copy), "s")
                                P.wait("sync", "s", smv)
                                dma("sync", out[s][tc * 512 + sub * 128:tc * 512 + sub * 128 + 128,
                                                   d * 128:d * 128 + 128],
                                    SR[:, 1024 + tb * 128:1024 + tb * 128 + 128], "st")
                                if sub + 2 <= 3:
                                    P.wait("tensor", "st")
                            g_ev[qd] = P.mark("v")
                            P.barrier(("vector",), sems=("st",))
                P.barrier()

        def emit_causal_head(p, hh):
            hs = hh * 64
            h = 2 * p + hh
            pieces = []
            for kc in range(NT):
                qlo = kc * 128
                for j in range(qlo // 512, 4):
                    pieces.append((kc, j, max(qlo, j * 512), j * 512 + 512))
            ev_marks = {}
            av_marks = {}
            for pi, (kc, j, a, b) in enumerate(pieces):
                bi = pi % 2
                ssl = slice(bi * 1024, bi * 1024 + (b - a))
                if pi >= 2:
                    P.wait("tensor", "s", ev_marks[pi - 2])
                P.inc("tensor", mm(PB[:, ssl], KP[hs:hs + 64, kc * 128:kc * 128 + 128],
                                   QP[hs:hs + 64, a:b], True, True), "t")
                tm = P.mark("t")
                P.wait("scalar", "t", tm)
                if pi >= 2:
                    P.wait("scalar", "t", av_marks[pi - 2])
                em = P.inc("scalar", act(PBUF[:, bi, 0:b - a], PB[:, ssl], AF.Exp,
                                         bias=NEGB[:, hh:hh + 1], scale=0.125), "s")
                ev_marks[pi] = em
                if a == kc * 128:
                    P.wait("vector", "s", em)
                    em2 = P.inc("vector", lambda E, bi=bi:
                                E.tensor_mul(PBUF[:, bi, 0:128], PBUF[:, bi, 0:128], triub[:, :]), "v")
                    P.wait("tensor", "v", em2)
                else:
                    P.wait("tensor", "s", em)
                P.inc("tensor", mm(PA[0:65, a:b], VAUG[:, kc, h * 65:h * 65 + 65],
                                   PBUF[:, bi, 0:b - a], kc == 0, kc == min(4 * j + 3, NT - 1)), "t")
                av_marks[pi] = P.mark("t")
            P.wait("vector", "t")
            P.inc("vector", lambda E: E.reciprocal(SR[32:33, :], PA[64:65, :]), "v")
            P.wait("scalar", "v")
            cm = P.inc("scalar", act(SRB[0:1, :], SR[32:33, :], AF.Copy), "s")
            sm = P.inc("scalar", act(NUM[0:64, :], PA[0:64, :], AF.Copy), "s")
            P.wait("tensor", "s", cm)
            for j in range(4):
                jsl = slice(j * 512, j * 512 + 512)
                P.inc("tensor", mm(PB[:, jsl], onesb[0:1, :], SRB[0:1, jsl], True, True), "t")
            P.wait("vector", "t")
            P.wait("vector", "s", sm)
            P.inc("vector", lambda E: E.tensor_mul(OHT[hs:hs + 64, p, :], NUM[0:64, :], PB[0:64, :]), "v")
            P.barrier(("tensor", "scalar"), sems=("v",))

        def emit_attn_b(s):
            emit_preln()
            emit_bvb(6)
            dma("sync", WVB[:, :, :], swbT[2].rearrange("(c p) o -> p c o", p=128), "ds")
            emit_vproj(("ds", P.mark("ds")))
            for p in range(8):
                mq = load_wslice(swbT[0], p, 0, slice(0, 128), False)
                mk = load_wslice(swbT[1], p, 0, slice(128, 256), False)
                emit_pair_qk(mq, mk, BIAST[:, p, 4:5], BIAST[:, p, 5:6])
                vv = P.inc("vector", lambda E: E.tensor_mul(QAK[:, :], QP[:, :], QP[:, :]), "v")
                P.wait("tensor", "v", vv)
                for hh in range(2):
                    hsl = slice(hh * 64, hh * 64 + 64)
                    reg = PA if hh == 0 else PB
                    for j in range(4):
                        jsl = slice(j * 512, j * 512 + 512)
                        P.inc("tensor", mm(reg[0:1, jsl], onesb[hsl, 0:1], QAK[hsl, jsl], True, True), "t")
                P.wait("vector", "t")
                P.inc("vector", lambda E: E.tensor_reduce(SR[0:1, 0:1], PA[0:1, :], AX.X, ALU.max), "v")
                vv = P.inc("vector", lambda E: E.tensor_reduce(SR[0:1, 1:2], PB[0:1, :], AX.X, ALU.max), "v")
                vv = P.inc("vector", lambda E: E.tensor_mul(QAK[:, :], KP[:, :], KP[:, :]), "v")
                P.wait("tensor", "v", vv)
                for hh in range(2):
                    hsl = slice(hh * 64, hh * 64 + 64)
                    reg = PA if hh == 0 else PB
                    for j in range(4):
                        jsl = slice(j * 512, j * 512 + 512)
                        P.inc("tensor", mm(reg[0:1, jsl], onesb[hsl, 0:1], QAK[hsl, jsl], True, True), "t")
                P.wait("vector", "t")
                P.inc("vector", lambda E: E.tensor_reduce(SR[0:1, 2:3], PA[0:1, :], AX.X, ALU.max), "v")
                P.inc("vector", lambda E: E.tensor_reduce(SR[0:1, 3:4], PB[0:1, :], AX.X, ALU.max), "v")
                vfence()
                P.inc("vector", lambda E: E.tensor_add(SR[0:1, 4:5], SR[0:1, 0:1], SR[0:1, 2:3]), "v")
                vfence()
                P.inc("vector", lambda E: E.tensor_add(SR[0:1, 5:6], SR[0:1, 1:2], SR[0:1, 3:4]), "v")
                P.wait("scalar", "v")
                P.inc("scalar", act(SRB[0:1, 0:2], SR[0:1, 4:6], AF.Copy, scale=1.0 / 16.0), "s")
                P.wait("tensor", "s")
                P.inc("tensor", mm(PA[:, 0:1], onesb[0:1, :], SRB[0:1, 0:1], True, True), "t")
                P.inc("tensor", mm(PA[:, 1:2], onesb[0:1, :], SRB[0:1, 1:2], True, True), "t")
                P.wait("scalar", "t")
                P.inc("scalar", act(NEGB[:, 0:2], PA[:, 0:2], AF.Copy, scale=-1.0), "s")
                P.barrier(("tensor", "vector", "scalar"), sems=("s", "t", "v"))
                emit_causal_head(p, 0)
                emit_causal_head(p, 1)
                P.barrier()
            emit_out_proj(swbT[3], 7, False)

        def emit_exit(s):
            YX = Y[:, :, :].rearrange("p a b -> p (a b)").rearrange("p (tt c j) -> p tt c j", tt=NT, c=NC)
            for c in range(NC):
                dmat(YX[:, :, c, :], XT[:, c, :])
            P.wait("gpsimd", "ds")
            dma("gpsimd", out[s].rearrange("(tt p) (c j) -> p tt c j", p=128, j=128), YX[:, :, :, :], "st")
            P.barrier()

        for s in range(nseq):
            emit_entry(s)
            if stop == "entry":
                emit_exit(s)
                continue
            emit_attn_a(s)
            if stop == "attn_a":
                emit_exit(s)
                continue
            emit_ffn(s, dbase=6 if (dbgmode and s == 0) else None)
            if stop == "ffn1":
                emit_exit(s)
                continue
            emit_attn_b(s)
            if stop == "attn_b":
                emit_exit(s)
                continue
            emit_ffn(s, final=True)

        P.wait("gpsimd", "st")
        P.wait("sync", "st")

        with nc.Block() as block:
            prog.replay(block)
    return nc


def make_inputs(inputs):
    q = np.ascontiguousarray(inputs["L0_attn_a_queries"], dtype=np.float32)
    bias = np.stack([
        inputs["L0_attn_a_bq"], inputs["L0_attn_a_bk"], inputs["L0_attn_a_bv"], inputs["L0_attn_a_bo"],
        inputs["L0_attn_b_bq"], inputs["L0_attn_b_bk"], inputs["L0_attn_b_bv"], inputs["L0_attn_b_bo"],
    ]).astype(np.float32)
    lnwb = np.stack([inputs["L0_ffn_ln_w"], inputs["L0_ffn_ln_b"]]).astype(np.float32)
    shared = {
        "waq": np.ascontiguousarray(inputs["L0_attn_a_Wq"], dtype=np.float32),
        "wak": np.ascontiguousarray(inputs["L0_attn_a_Wk"], dtype=np.float32),
        "wav": np.ascontiguousarray(inputs["L0_attn_a_Wv"], dtype=np.float32),
        "wao": np.ascontiguousarray(inputs["L0_attn_a_Wo"], dtype=np.float32),
        "aproj": np.ascontiguousarray(inputs["L0_attn_a_proj"], dtype=np.float32),
        "wbq": np.ascontiguousarray(inputs["L0_attn_b_Wq"], dtype=np.float32),
        "wbk": np.ascontiguousarray(inputs["L0_attn_b_Wk"], dtype=np.float32),
        "wbv": np.ascontiguousarray(inputs["L0_attn_b_Wv"], dtype=np.float32),
        "wbo": np.ascontiguousarray(inputs["L0_attn_b_Wo"], dtype=np.float32),
        "biases": bias,
        "wfc": np.ascontiguousarray(inputs["L0_ffn_W_fc"], dtype=np.float32),
        "wproj": np.ascontiguousarray(inputs["L0_ffn_W_proj"], dtype=np.float32),
        "lnwb": lnwb,
        "eye_f": np.eye(128, dtype=np.float32),
        "ones_b": np.ones((128, 128), dtype=ml_dtypes.bfloat16),
        "ones_f": np.ones((1, 128), dtype=np.float32),
        "triu_b": np.triu(np.ones((128, 128))).astype(ml_dtypes.bfloat16),
    }
    in_maps = []
    for c in range(NCORE):
        m = dict(shared)
        m["xq"] = np.ascontiguousarray(q[c * NSEQ:(c + 1) * NSEQ])
        in_maps.append(m)
    return in_maps


def kernel(**inputs):
    nc = build_program()
    in_maps = make_inputs(inputs)
    res = run_bass_kernel_spmd(nc, in_maps, core_ids=list(range(NCORE)))
    return np.concatenate([np.asarray(r["out"], dtype=np.float32) for r in res.results], axis=0)


# version 11
# speedup vs baseline: 1.2064x; 1.2064x over previous
import math
import contextlib
import numpy as np
import ml_dtypes
import concourse.bass as bass
import concourse.mybir as mybir
from concourse.bass_utils import run_bass_kernel_spmd

F32 = mybir.dt.float32
BF16 = mybir.dt.bfloat16
AF = mybir.ActivationFunctionType
ALU = mybir.AluOpType
AX = mybir.AxisListType

NCORE = 8
NSEQ = 4
T = 2048
D = 1024
NC = D // 128
NT = T // 128
NHEAD = 16
DH = 64
NF = 256
HID = 4096
NH = HID // 128
EPS_F = 1e-6
LN_EPS = 1e-5
RATIO2 = 1.0 / NF
EPS_NP = 1e-6 / RATIO2
DN = 1.0 / math.sqrt(math.sqrt(DH))
GAMMA_DEN = 1.0 / (HID * D)
ENGINES = ("tensor", "vector", "scalar", "gpsimd", "sync")


class Prog:

    def __init__(self, nc):
        self.nc = nc
        self.q = {e: [] for e in ENGINES}
        self.semh = {}
        self.semv = {}
        self.twait = {}
        self.warm_ap = None

    def sem(self, name, ctx):
        self.semh[name] = ctx.enter_context(self.nc.semaphore(name))
        self.semv[name] = 0

    def emit(self, eng, fn):
        self.q[eng].append(fn)

    def inc(self, eng, fn, sem, n=1):
        self.semv[sem] += n
        h = self.semh[sem]
        self.q[eng].append(lambda E, fn=fn, h=h, n=n: fn(E).then_inc(h, n))
        return self.semv[sem]

    def mark(self, sem):
        return self.semv[sem]

    def wait(self, eng, sem, val=None):
        v = self.semv[sem] if val is None else val
        if v <= 0:
            return
        h = self.semh[sem]
        if eng == "tensor" and self.warm_ap is not None:
            last = self.twait.get(sem, 0)
            step = 16 if sem in ("ds", "dg", "st") else 1
            span = (v - last) // step
            if span >= 4:
                nf = min(6, span - 1)
                w = self.warm_ap
                for i in range(1, nf + 1):
                    u = last + (span * i // (nf + 1)) * step
                    if u <= last or u >= v:
                        continue
                    self.q[eng].append(lambda E, h=h, u=u: E.wait_ge(h, u))
                    self.q[eng].append(lambda E, w=w: E.ldweights(w))
            if v > last:
                self.twait[sem] = v
        self.q[eng].append(lambda E, h=h, v=v: E.wait_ge(h, v))

    def barrier(self, engines=ENGINES, sems=("t", "v", "s", "ds", "dg", "st")):
        vals = [(s, self.semv[s]) for s in sems if self.semv[s] > 0]
        for e in engines:
            for s, v in vals:
                self.wait(e, s, v)

    def replay(self, block):
        for name in ENGINES:
            def make(qn):
                def run(E):
                    for fn in self.q[qn]:
                        fn(E)
                return run
            getattr(block, name)(make(name))


def build_program(stop=None, nseq=NSEQ, dbgmode=False):
    nc = bass.Bass("TRN2", target_bir_lowering=False)

    xq = nc.dram_tensor("xq", [NSEQ, T, D], F32, kind="ExternalInput").ap()
    wa = {k: nc.dram_tensor(f"wa{k}", [D, D], F32, kind="ExternalInput").ap() for k in "qkvo"}
    aproj = nc.dram_tensor("aproj", [DH, NF], F32, kind="ExternalInput").ap()
    wb = {k: nc.dram_tensor(f"wb{k}", [D, D], F32, kind="ExternalInput").ap() for k in "qkvo"}
    biases = nc.dram_tensor("biases", [8, D], F32, kind="ExternalInput").ap()
    wfc = nc.dram_tensor("wfc", [HID, D], F32, kind="ExternalInput").ap()
    wproj = nc.dram_tensor("wproj", [D, HID], F32, kind="ExternalInput").ap()
    lnwb = nc.dram_tensor("lnwb", [2, HID], F32, kind="ExternalInput").ap()
    eye_f = nc.dram_tensor("eye_f", [128, 128], F32, kind="ExternalInput").ap()
    ones_b = nc.dram_tensor("ones_b", [128, 128], BF16, kind="ExternalInput").ap()
    ones_f = nc.dram_tensor("ones_f", [1, 128], F32, kind="ExternalInput").ap()
    triu_b = nc.dram_tensor("triu_b", [128, 128], BF16, kind="ExternalInput").ap()
    out = nc.dram_tensor("out", [NSEQ, T, D], F32, kind="ExternalOutput").ap()

    dbg = {i: nc.dram_tensor(f"dbg{i}", [128, 47492], BF16, kind="ExternalOutput").ap() for i in range(9)}
    dbgf = nc.dram_tensor("dbgf", [128, 64], F32, kind="ExternalOutput").ap()
    swbT = nc.dram_tensor("swbT", [4, D, D], BF16).ap()
    swfcqT = nc.dram_tensor("swfcqT", [NH, NC, 128, 128], BF16).ap()
    swpT = nc.dram_tensor("swpT", [4, HID, 256], BF16).ap()

    ctx = contextlib.ExitStack()
    with ctx:
        sbt = lambda *a, **k: ctx.enter_context(nc.sbuf_tensor(*a, **k))
        pst = lambda *a, **k: ctx.enter_context(nc.psum_tensor(*a, **k))

        XT = sbt("XT", [128, NC, T], BF16, is_transpose_dest=True)
        Y = sbt("Y", [128, NC, T], BF16, is_transpose_dest=True)
        AR = sbt("AR", [128, 47492], BF16)
        WS = sbt("WS", [128, 2, 8, 256], BF16)
        RB = sbt("RB", [128, T], BF16)
        MRB = sbt("MRB", [128, T], BF16)
        SR = sbt("SR", [128, 2048], F32)
        SRB = sbt("SRB", [128, 2048], BF16)
        CONS = sbt("CONS", [128, 2, 128], BF16)
        EYEF = sbt("EYEF", [128, 128], F32)
        ONEF = sbt("ONEF", [1, 128], F32)
        BIAST = sbt("BIAST", [128, 8, 8], F32)
        LNC = sbt("LNC", [128, NH, 2], BF16)
        LNCF = sbt("LNCF", [128, NH], F32)
        PAUG = sbt("PAUG", [65, NF], BF16)
        S2C = sbt("S2C", [128, NC], F32)
        S1R = sbt("S1R", [1, D], F32)
        RBF = sbt("RBF", [128, 512], F32)
        FFS = sbt("FFS", [128, 1024], F32)
        THR = sbt("THR", [128, 4], F32)
        ACC = sbt("ACC", [128, 2], F32)
        NEGB = sbt("NEGB", [128, 2], F32)
        BVB = sbt("BVB", [128, D], BF16)
        CB = sbt("CB", [128, 4], F32)

        onesb = CONS[:, 0, :]
        triub = CONS[:, 1, :]
        ARF = AR[:, :]

        def arv(off, shape):
            n = int(np.prod(shape[1:]))
            ap = ARF[:, off:off + n]
            if len(shape) == 3:
                ap = ap.rearrange("p (a b) -> p a b", a=shape[1])
            elif len(shape) == 4:
                ap = ap.rearrange("p (a b c) -> p a b c", a=shape[1], b=shape[2])
            return ap

        VAUG = arv(0, [128, NT, NHEAD * 65])
        OHT = arv(16640, [128, NC, T])
        QP = arv(33024, [128, T])
        KP = arv(35072, [128, T])
        QAK = arv(37120, [128, T])
        KPR = arv(39168, [128, NT, NF])
        QPR = arv(43264, [128, 2, T])
        KVA = arv(47360, [128, 2, 65])
        NUM = arv(37120, [128, T])
        QSQ = arv(39168, [128, T])
        PBUF = arv(39168, [128, 2, 512])
        XINV = arv(16640, [128, NT, D])
        WVB = arv(16640, [128, NC, D])
        HB = arv(0, [128, NH, 512])
        WQS = arv(16640, [128, 2, NH, 256])
        FSQ = arv(33024, [128, 2, 512])
        FT1 = arv(34048, [128, 512])
        WFS = arv(35072, [128, 2, 8, 128])
        XSQ = arv(0, [128, 2, 512])
        PATMP = arv(0, [128, 128])
        PSTG = arv(128, [128, 2, 128])
        PQS = arv(384, [128, 2, 128])

        PA = pst("PA", [128, 2048], F32)
        PB = pst("PB", [128, 2048], F32)

        prog = Prog(nc)
        for s_ in ("t", "v", "s", "ds", "dg", "st"):
            prog.sem(s_, ctx)
        P = prog
        P.warm_ap = CONS[:, 0, :]

        def vfence():
            P.wait("vector", "v")

        def sfence():
            P.wait("scalar", "s")

        def dma(q, dst, src, sem):
            return P.inc(q, lambda E, d=dst, s=src: E.dma_start(out=d, in_=s), sem, 16)

        def dump(i, buf):
            if not dbgmode:
                return
            P.barrier()
            n = int(np.prod(buf.shape[1:]))
            dma("gpsimd", dbg[i][:, 0:n], buf.rearrange("p a b -> p (a b)") if len(buf.shape) == 3 else buf, "st")
            P.barrier()

        def dmat(dst, src):
            return P.inc("sync", lambda E, d=dst, s=src: E.dma_start_transpose(d, s), "ds", 16)

        def mm(o, l, r, st_, sp):
            return lambda E, o=o, l=l, r=r, a=st_, b=sp: E.matmul(o, l, r, start=a, stop=b)

        def act(o, i, f, bias=None, scale=None):
            kw = {}
            if bias is not None:
                kw["bias"] = bias
            if scale is not None:
                kw["scale"] = scale
            return lambda E, o=o, i=i, f=f, kw=kw: E.activation(o, i, f, **kw)

        def tp(o, i, np_=128):
            return lambda E, o=o, i=i, np_=np_: E.transpose(o, i, EYEF[0:np_, 0:np_])

        P.inc("vector", lambda E: E.memset(CB[:, 0:1], LN_EPS), "v")
        P.inc("vector", lambda E: E.memset(CB[:, 1:2], 0.5e-6), "v")
        P.inc("vector", lambda E: E.memset(CB[:, 2:3], EPS_NP), "v")
        dma("sync", CONS[:, 0, :], ones_b, "ds")
        dma("sync", CONS[:, 1, :], triu_b, "ds")
        dma("sync", EYEF[:, :], eye_f, "ds")
        dma("sync", ONEF[:, :], ones_f, "ds")
        dma("gpsimd", SR[0:8, 0:1024], biases, "dg")
        P.wait("tensor", "ds")
        P.wait("tensor", "dg")
        ev_marks = {}
        for oc in range(8):
            slot = (oc % 2) * 512
            if oc >= 2:
                P.wait("tensor", "s", ev_marks[oc - 2])
            P.inc("tensor", tp(PA[:, slot:slot + 8], SR[0:8, oc * 128:oc * 128 + 128], 8), "t")
            P.wait("scalar", "t")
            ev_marks[oc] = P.inc("scalar", act(BIAST[:, oc, :], PA[:, slot:slot + 8], AF.Copy), "s")
        for half in range(2):
            P.wait("sync", "s")
            dma("sync", SR[0:2, 0:2048], lnwb[:, half * 2048:half * 2048 + 2048], "ds")
            P.wait("tensor", "ds")
            ev_marks = {}
            for j in range(16):
                ht = half * 16 + j
                slot = (j % 2) * 512
                if j >= 2:
                    P.wait("tensor", "s", ev_marks[j - 2])
                P.inc("tensor", tp(PA[:, slot:slot + 2], SR[0:2, j * 128:j * 128 + 128], 2), "t")
                P.wait("scalar", "t")
                P.inc("scalar", act(LNC[:, ht, :], PA[:, slot:slot + 2], AF.Copy), "s")
                ev_marks[j] = P.inc("scalar", act(LNCF[:, ht:ht + 1], PA[:, slot:slot + 1], AF.Copy), "s")
        P.wait("gpsimd", "s")
        dma("gpsimd", SR[0:64, 0:256], aproj, "dg")
        P.wait("vector", "dg")
        P.inc("vector", lambda E: E.tensor_scalar_mul(PAUG[0:64, :], SR[0:64, 0:256], DN), "v")
        P.inc("vector", lambda E: E.memset(PAUG[64:65, :], -1.0), "v")
        P.barrier()

        for i, k in enumerate("qkvo"):
            blk_ev = {}
            blk_st = {}
            ld_done = {}
            for ot in range(8):
                sl = (ot % 2) * 1024
                if ot >= 2:
                    P.wait("gpsimd", "t", ld_done[ot - 2])
                dma("gpsimd", SR[:, sl:sl + 1024], wb[k][ot * 128:ot * 128 + 128, :], "dg")
                P.wait("tensor", "dg")
                for ct in range(8):
                    bi = ct % 2
                    idx = ot * 8 + ct
                    if idx >= 2:
                        P.wait("tensor", "s", blk_ev[(idx - 2) % 4])
                    P.inc("tensor", tp(PA[:, bi * 512:bi * 512 + 128],
                                       SR[:, sl + ct * 128:sl + ct * 128 + 128]), "t")
                    P.wait("scalar", "t")
                    if idx >= 2:
                        P.wait("scalar", "ds", blk_st[(idx - 2) % 4])
                    blk_ev[idx % 4] = P.inc("scalar", act(PSTG[:, bi, :], PA[:, bi * 512:bi * 512 + 128],
                                                          AF.Copy), "s")
                    P.wait("sync", "s", blk_ev[idx % 4])
                    blk_st[idx % 4] = dma("sync", swbT[i, ct * 128:ct * 128 + 128, ot * 128:ot * 128 + 128],
                                          PSTG[:, bi, :], "ds")
                ld_done[ot] = P.mark("t")
            P.barrier()

        P.inc("vector", lambda E: E.memset(ACC[:, 0:1], 0.0), "v")
        ld_done = {}
        for it in range(NH):
            sl = (it % 2) * 1024
            if it >= 2:
                P.wait("gpsimd", "v", ld_done[it - 2])
            dma("gpsimd", SR[:, sl:sl + 1024], wfc[it * 128:it * 128 + 128, :], "dg")
            P.wait("vector", "dg")
            vfence()
            P.inc("vector", lambda E, sl=sl: E.tensor_reduce(ACC[:, 1:2], SR[:, sl:sl + 1024], AX.X, ALU.add,
                                                             apply_absolute_value=True), "v")
            vfence()
            ld_done[it] = P.inc("vector", lambda E: E.tensor_add(ACC[:, 0:1], ACC[:, 0:1], ACC[:, 1:2]), "v")

        def emit_thr(dbgcol=None):
            if dbgmode and dbgcol is not None:
                P.barrier()
                dma("gpsimd", dbgf[:, dbgcol:dbgcol + 2], ACC[:, :], "st")
                P.barrier()
            P.wait("tensor", "v")
            P.inc("tensor", tp(PA[0:1, 0:128], ACC[:, 0:1]), "t")
            P.wait("vector", "t")
            P.inc("vector", lambda E: E.tensor_reduce(SR[0:1, 0:1], PA[0:1, 0:128], AX.X, ALU.add), "v")
            vfence()
            P.inc("vector", lambda E: E.tensor_scalar(SR[0:1, 1:2], SR[0:1, 0:1], 0.5 * GAMMA_DEN, 0.5e-6,
                                                      ALU.mult, ALU.add), "v")
            P.wait("tensor", "v")
            P.inc("tensor", mm(PA[:, 0:1], ONEF[0:1, :], SR[0:1, 1:2], True, True), "t")
            P.wait("scalar", "t")
            P.inc("scalar", act(THR[:, 0:1], PA[:, 0:1], AF.Copy), "s")
            P.inc("scalar", act(THR[:, 1:2], PA[:, 0:1], AF.Copy, scale=-1.0), "s")
            if dbgmode and dbgcol is not None:
                P.barrier()
                dma("gpsimd", dbgf[0:1, dbgcol + 2:dbgcol + 6], SR[0:1, 0:4], "st")
                dma("gpsimd", dbgf[:, dbgcol + 6:dbgcol + 8], THR[:, 0:2], "st")
                P.barrier()
            P.barrier()
        emit_thr(dbgcol=16)

        def quant_ops(src, dst, lnw_ap=None, unscaled=None):
            vfence()
            P.inc("vector", lambda E, s=src: E.tensor_single_scalar(PATMP[:, :], s, THR[:, 1:2], ALU.is_le), "v")
            vfence()
            if lnw_ap is None:
                return P.inc("vector", lambda E, s=src, d=dst:
                             E.scalar_tensor_tensor(d, s, THR[:, 0:1], PATMP[:, :], ALU.is_ge, ALU.subtract), "v")
            P.inc("vector", lambda E, s=src, d=unscaled:
                  E.scalar_tensor_tensor(d, s, THR[:, 0:1], PATMP[:, :], ALU.is_ge, ALU.subtract), "v")
            vfence()
            return P.inc("vector", lambda E, d=dst, e=unscaled, l=lnw_ap:
                         E.tensor_single_scalar(d, e, l, ALU.mult), "v")

        blk_q = {}
        blk_st = {}
        ld_done = {}
        for it in range(NH):
            sl = (it % 2) * 1024
            if it >= 2:
                P.wait("gpsimd", "t", ld_done[it - 2])
            dma("gpsimd", SR[:, sl:sl + 1024], wfc[it * 128:it * 128 + 128, :], "dg")
            P.wait("tensor", "dg")
            for ct in range(8):
                bi = ct % 2
                idx = it * 8 + ct
                if idx >= 2:
                    P.wait("tensor", "v", blk_q[(idx - 2) % 4])
                P.inc("tensor", tp(PA[:, bi * 512:bi * 512 + 128],
                                   SR[:, sl + ct * 128:sl + ct * 128 + 128]), "t")
                P.wait("vector", "t")
                if idx >= 2:
                    P.wait("vector", "ds", blk_st[(idx - 2) % 4])
                blk_q[idx % 4] = quant_ops(PA[:, bi * 512:bi * 512 + 128], PSTG[:, bi, :])
                P.wait("sync", "v", blk_q[idx % 4])
                blk_st[idx % 4] = dma("sync", swfcqT[it, ct], PSTG[:, bi, :], "ds")
            ld_done[it] = P.mark("t")
        P.barrier()

        P.inc("vector", lambda E: E.memset(ACC[:, 0:1], 0.0), "v")
        ld_done = {}
        for it in range(32):
            dt, ch = it // 4, it % 4
            sl = (it % 2) * 1024
            if it >= 2:
                P.wait("gpsimd", "v", ld_done[it - 2])
            dma("gpsimd", SR[:, sl:sl + 1024], wproj[dt * 128:dt * 128 + 128, ch * 1024:ch * 1024 + 1024], "dg")
            P.wait("vector", "dg")
            vfence()
            P.inc("vector", lambda E, sl=sl: E.tensor_reduce(ACC[:, 1:2], SR[:, sl:sl + 1024], AX.X, ALU.add,
                                                             apply_absolute_value=True), "v")
            vfence()
            ld_done[it] = P.inc("vector", lambda E: E.tensor_add(ACC[:, 0:1], ACC[:, 0:1], ACC[:, 1:2]), "v")
        emit_thr(dbgcol=24)

        for dt in range(8):
            for rnd in range(2):
                P.wait("gpsimd", "t")
                P.wait("gpsimd", "v")
                dma("gpsimd", SR[:, 0:1024], wproj[dt * 128:dt * 128 + 128, rnd * 2048:rnd * 2048 + 1024], "dg")
                dma("gpsimd", SR[:, 1024:2048],
                    wproj[dt * 128:dt * 128 + 128, rnd * 2048 + 1024:rnd * 2048 + 2048], "dg")
                P.wait("tensor", "dg")
                blk_q = {}
                blk_st = {}
                blk_mm = {}
                for j in range(16):
                    ht = rnd * 16 + j
                    bi = j % 2
                    if j >= 2:
                        P.wait("tensor", "v", blk_q[(j - 2) % 4])
                    P.inc("tensor", tp(PA[:, bi * 512:bi * 512 + 128], SR[:, j * 128:j * 128 + 128]), "t")
                    P.wait("vector", "t")
                    if j >= 2:
                        P.wait("vector", "ds", blk_st[(j - 2) % 4])
                        P.wait("vector", "t", blk_mm[(j - 2) % 4])
                    blk_q[j % 4] = quant_ops(PA[:, bi * 512:bi * 512 + 128], PSTG[:, bi, :],
                                             lnw_ap=LNCF[:, ht:ht + 1], unscaled=PQS[:, bi, :])
                    P.wait("tensor", "v", blk_q[j % 4])
                    blk_mm[j % 4] = P.inc("tensor", mm(PB[:, 0:2], PQS[:, bi, :], LNC[:, ht, :],
                                                       ht == 0, ht == NH - 1), "t")
                    P.wait("sync", "v", blk_q[j % 4])
                    blk_st[j % 4] = dma("sync",
                                        swpT[dt // 2, ht * 128:ht * 128 + 128, (dt % 2) * 128:(dt % 2) * 128 + 128],
                                        PSTG[:, bi, :], "ds")
            P.wait("scalar", "t")
            P.inc("scalar", act(S2C[:, dt:dt + 1], PB[:, 1:2], AF.Copy), "s")
            P.inc("scalar", act(THR[:, 2:3], PB[:, 0:1], AF.Copy), "s")
            P.wait("tensor", "s")
            P.inc("tensor", tp(PA[0:1, 1024:1152], THR[:, 2:3]), "t")
            P.wait("scalar", "t")
            P.inc("scalar", act(S1R[0:1, dt * 128:dt * 128 + 128], PA[0:1, 1024:1152], AF.Copy, scale=-1.0), "s")
            P.barrier(("tensor",), sems=("s",))
        P.barrier()

        if dbgmode:
            P.barrier()
            dma("gpsimd", dbgf[:, 8:12], THR[:, :], "st")
            for _i in range(NH):
                dma("gpsimd", dbg[3][:, _i * 1024:(_i + 1) * 1024].rearrange("p (b j) -> p b j", b=NC),
                    swfcqT[_i].rearrange("b p j -> p b j"), "st")
            P.barrier()
        if stop == "preproc":
            nseq = 0

        def emit_entry(s):
            dma("gpsimd", XINV[:, :, :], xq[s].rearrange("(tt p) d -> p tt d", p=128), "dg")
            P.wait("sync", "dg")
            for tt in range(NT):
                dmat(XT[:, :, tt * 128:tt * 128 + 128], XINV[:, tt, :])
            P.barrier()

        def emit_preln():
            sq_used = {}
            for tc in range(4):
                tsl = slice(tc * 512, tc * 512 + 512)
                for c in range(NC):
                    bi = c % 2
                    idx = tc * NC + c
                    if idx >= 2:
                        P.wait("vector", "t", sq_used[(idx - 2) % 4])
                    v = P.inc("vector", lambda E, c=c, bi=bi, tsl=tsl:
                              E.tensor_mul(XSQ[:, bi, :], XT[:, c, tsl], XT[:, c, tsl]), "v")
                    P.inc("tensor", mm(PB[0:1, tsl], onesb[:, 0:1], XT[:, c, tsl], c == 0, c == NC - 1), "t")
                    P.wait("tensor", "v", v)
                    sq_used[idx % 4] = P.inc("tensor", mm(PA[0:1, tsl], onesb[:, 0:1], XSQ[:, bi, :],
                                                          c == 0, c == NC - 1), "t")
            P.wait("scalar", "t")
            for tc in range(4):
                a = tc * 512
                M_, Q_, T_, R_ = SR[0:1, 0:512], SR[0:1, 512:1024], SR[0:1, 1024:1536], SR[0:1, 1536:2048]
                P.inc("scalar", act(M_, PB[0:1, a:a + 512], AF.Copy, scale=1.0 / D), "s")
                P.inc("scalar", act(Q_, PA[0:1, a:a + 512], AF.Copy, scale=1.0 / D), "s")
                P.wait("vector", "s")
                P.inc("vector", lambda E, M_=M_, T_=T_: E.tensor_mul(T_, M_, M_), "v")
                vfence()
                P.inc("vector", lambda E, Q_=Q_, T_=T_: E.tensor_sub(T_, Q_, T_), "v")
                P.wait("scalar", "v")
                P.inc("scalar", act(R_, T_, AF.Ln, bias=CB[0:1, 0:1]), "s")
                sfence()
                P.inc("scalar", act(R_, R_, AF.Exp, scale=-0.5), "s")
                P.wait("vector", "s")
                P.inc("vector", lambda E, M_=M_, R_=R_: E.tensor_mul(M_, M_, R_), "v")
                P.wait("scalar", "v")
                P.inc("scalar", act(SRB[0:1, a:a + 512], R_, AF.Copy), "s")
                sm_ = P.inc("scalar", act(RB[0:1, a:a + 512], M_, AF.Copy), "s")
                if tc < 3:
                    P.wait("scalar", "s", sm_)
            P.wait("tensor", "s")
            for j in range(4):
                jsl = slice(j * 512, j * 512 + 512)
                P.inc("tensor", mm(PA[:, jsl], onesb[0:1, :], SRB[0:1, jsl], True, True), "t")
                P.inc("tensor", mm(PB[:, jsl], onesb[0:1, :], RB[0:1, jsl], True, True), "t")
            P.wait("scalar", "t")
            P.inc("scalar", act(RB[:, :], PA[:, :], AF.Copy), "s")
            P.inc("scalar", act(MRB[:, :], PB[:, :], AF.Copy), "s")
            P.wait("vector", "s")
            nm = {}
            for t4 in range(4):
                t4sl = slice(t4 * 512, t4 * 512 + 512)
                for c in range(NC):
                    P.inc("vector", lambda E, c=c, t4sl=t4sl:
                          E.tensor_mul(Y[:, c, t4sl], XT[:, c, t4sl], RB[:, t4sl]), "v")
                vfence()
                for c in range(NC):
                    P.inc("vector", lambda E, c=c, t4sl=t4sl:
                          E.tensor_sub(Y[:, c, t4sl], Y[:, c, t4sl], MRB[:, t4sl]), "v")
                nm[t4] = P.mark("v")
            P.wait("tensor", "s")
            return nm

        def emit_bvb(bias_dg_mark):
            P.wait("tensor", "dg", bias_dg_mark)
            P.inc("tensor", mm(PA[:, 0:512], onesb[32:33, :], SRB[32:33, 0:512], True, True), "t")
            P.inc("tensor", mm(PA[:, 512:1024], onesb[32:33, :], SRB[32:33, 512:1024], True, True), "t")
            P.wait("scalar", "t")
            bm = P.inc("scalar", act(BVB[:, :], PA[:, 0:1024], AF.Copy), "s")
            P.wait("tensor", "s", bm)
            P.wait("vector", "s", bm)

        def emit_vproj(wmark, nm):
            P.wait("tensor", wmark[0], wmark[1])
            ev = {}
            for tt in range(NT):
                tsl = slice(tt * 128, tt * 128 + 128)
                reg = PA if tt % 2 == 0 else PB
                if tt % 4 == 0:
                    P.wait("tensor", "v", nm[tt // 4])
                if tt >= 2:
                    P.wait("tensor", "v", ev[tt - 2])
                for c in range(NC):
                    P.inc("tensor", mm(reg[:, 0:512], Y[:, c, tsl], WVB[:, c, 0:512], c == 0, c == NC - 1), "t")
                    P.inc("tensor", mm(reg[:, 512:1024], Y[:, c, tsl], WVB[:, c, 512:1024], c == 0, c == NC - 1), "t")
                P.wait("vector", "t")
                ev[tt] = P.inc("vector", lambda E, tt=tt, reg=reg:
                               E.tensor_add(VAUG[:, tt, :].rearrange("p (h e) -> p h e", h=NHEAD)[:, :, 0:64],
                                            reg[:, 0:1024].rearrange("p (h e) -> p h e", h=NHEAD),
                                            BVB[:, :].rearrange("p (h e) -> p h e", h=NHEAD)), "v")
            for tt in range(NT):
                P.inc("vector", lambda E, tt=tt:
                      E.memset(VAUG[:, tt, :].rearrange("p (h e) -> p h e", h=NHEAD)[:, :, 64:65], 1.0), "v")
            P.barrier()

        def load_wslice(wsrc, o, buf, colsl, cast, q=None):
            src = wsrc[:, o * 128:o * 128 + 128].rearrange("(c p) o -> p c o", p=128)
            if cast or q is None:
                q = "gpsimd" if cast else "sync"
            sem = "dg" if q == "gpsimd" else "ds"
            dma(q, WS[:, buf, :, colsl], src, sem)
            return (sem, P.mark(sem))

        def emit_pair_qk(mq, mk, bq_ap, bk_ap):
            P.wait("tensor", mq[0], mq[1])
            for c in range(NC):
                for j in range(4):
                    jsl = slice(j * 512, j * 512 + 512)
                    P.inc("tensor", mm(PA[:, jsl], WS[:, 0, c, 0:128], Y[:, c, jsl], c == 0, c == NC - 1), "t")
            P.wait("vector", "t")
            for j_ in range(2):
                P.inc("vector", lambda E, b=bq_ap, j_=j_:
                      E.tensor_single_scalar(QP[:, j_ * 1024:j_ * 1024 + 1024],
                                             PA[:, j_ * 1024:j_ * 1024 + 1024], b, ALU.add), "v")
            P.wait("tensor", mk[0], mk[1])
            for c in range(NC):
                for j in range(4):
                    jsl = slice(j * 512, j * 512 + 512)
                    P.inc("tensor", mm(PB[:, jsl], WS[:, 0, c, 128:256], Y[:, c, jsl], c == 0, c == NC - 1), "t")
            P.wait("vector", "t")
            for j_ in range(2):
                P.inc("vector", lambda E, b=bk_ap, j_=j_:
                      E.tensor_single_scalar(KP[:, j_ * 1024:j_ * 1024 + 1024],
                                             PB[:, j_ * 1024:j_ * 1024 + 1024], b, ALU.add), "v")
            P.barrier(("tensor", "vector", "scalar"), sems=("s", "v"))

        def emit_perf_head(p, hh):
            hs = hh * 64
            h = 2 * p + hh
            P.inc("vector", lambda E: E.tensor_copy(QAK[0:64, :], QP[hs:hs + 64, :]), "v")
            vv = P.inc("vector", lambda E: E.tensor_mul(QSQ[hs:hs + 64, :], QP[hs:hs + 64, :], QP[hs:hs + 64, :]), "v")
            P.wait("tensor", "v", vv)
            for j in range(4):
                jsl = slice(j * 512, j * 512 + 512)
                P.inc("tensor", mm(PA[0:1, jsl], onesb[hs:hs + 64, 0:1], QSQ[hs:hs + 64, jsl], True, True), "t")
            P.wait("scalar", "t")
            P.inc("scalar", act(QAK[64:65, :], PA[0:1, :], AF.Copy, scale=DN * DN * 0.5), "s")
            P.wait("tensor", "s")
            for fc in range(2):
                reg = PA if fc == 0 else PB
                for j in range(4):
                    jsl = slice(j * 512, j * 512 + 512)
                    P.inc("tensor", mm(reg[:, jsl], PAUG[:, fc * 128:fc * 128 + 128], QAK[0:65, jsl], True, True), "t")
            P.wait("scalar", "t")
            for fc in range(2):
                reg = PA if fc == 0 else PB
                for j in range(4):
                    jsl = slice(j * 512, j * 512 + 512)
                    P.inc("scalar", act(QPR[:, fc, jsl], reg[:, jsl], AF.Exp), "s")
            P.wait("vector", "s")
            for fc in range(2):
                for j in range(2):
                    P.inc("vector", lambda E, fc=fc, j=j:
                          E.tensor_scalar_add(QPR[:, fc, j * 1024:j * 1024 + 1024],
                                              QPR[:, fc, j * 1024:j * 1024 + 1024], EPS_F), "v")
            P.inc("vector", lambda E: E.tensor_copy(QAK[0:64, :], KP[hs:hs + 64, :]), "v")
            vv = P.inc("vector", lambda E: E.tensor_mul(QSQ[hs:hs + 64, :], KP[hs:hs + 64, :], KP[hs:hs + 64, :]), "v")
            P.wait("tensor", "v", vv)
            for j in range(4):
                jsl = slice(j * 512, j * 512 + 512)
                P.inc("tensor", mm(PA[0:1, jsl], onesb[hs:hs + 64, 0:1], QSQ[hs:hs + 64, jsl], True, True), "t")
            P.wait("scalar", "t")
            P.inc("scalar", act(QAK[64:65, :], PA[0:1, :], AF.Copy, scale=DN * DN * 0.5), "s")
            P.wait("tensor", "s")
            kev = {}
            for tt in range(NT):
                slot = (tt % 4) * 512
                if tt >= 4:
                    P.wait("tensor", "s", kev[tt - 4])
                P.inc("tensor", mm(PA[:, slot:slot + 256], QAK[0:65, tt * 128:tt * 128 + 128], PAUG[:, :],
                                   True, True), "t")
                P.wait("scalar", "t")
                kev[tt] = P.inc("scalar", act(KPR[:, tt, :], PA[:, slot:slot + 256], AF.Exp), "s")
            P.wait("vector", "s")
            for q_ in range(4):
                vv = P.inc("vector", lambda E, q_=q_:
                           E.tensor_scalar_add(KPR[:, q_ * 4:q_ * 4 + 4, :],
                                               KPR[:, q_ * 4:q_ * 4 + 4, :], EPS_F), "v")
            P.wait("tensor", "v", vv)
            for tt in range(NT):
                P.inc("tensor", mm(PB[0:65, 0:256], VAUG[:, tt, h * 65:h * 65 + 65], KPR[:, tt, :],
                                   tt == 0, tt == NT - 1), "t")
            P.wait("scalar", "t")
            P.inc("scalar", act(SR[0:65, 0:256], PB[0:65, 0:256], AF.Copy), "s")
            P.wait("tensor", "s")
            for fc in range(2):
                P.inc("tensor", tp(PB[:, 1024 + fc * 256:1024 + fc * 256 + 65],
                                   SR[0:65, fc * 128:fc * 128 + 128], 65), "t")
            P.wait("scalar", "t")
            for fc in range(2):
                P.inc("scalar", act(KVA[:, fc, :], PB[:, 1024 + fc * 256:1024 + fc * 256 + 65], AF.Copy), "s")
            P.wait("tensor", "s")
            for fc in range(2):
                for j in range(4):
                    jsl = slice(j * 512, j * 512 + 512)
                    P.inc("tensor", mm(PA[0:65, jsl], KVA[:, fc, :], QPR[:, fc, jsl], fc == 0, fc == 1), "t")
            P.wait("scalar", "t")
            lm = P.inc("scalar", act(SR[0:1, :], PA[64:65, :], AF.Ln, bias=CB[0:1, 2:3]), "s")
            P.wait("scalar", "s", lm)
            cm = P.inc("scalar", act(SRB[0:1, :], SR[0:1, :], AF.Exp, scale=-1.0), "s")
            sm = P.inc("scalar", act(NUM[0:64, :], PA[0:64, :], AF.Copy), "s")
            P.wait("tensor", "s", cm)
            for j in range(4):
                jsl = slice(j * 512, j * 512 + 512)
                P.inc("tensor", mm(PB[:, jsl], onesb[0:1, :], SRB[0:1, jsl], True, True), "t")
            P.wait("vector", "t")
            P.wait("vector", "s", sm)
            P.inc("vector", lambda E: E.tensor_mul(OHT[hs:hs + 64, p, :], NUM[0:64, :], PB[0:64, :]), "v")
            P.barrier(("tensor", "scalar"), sems=("v",))

        def emit_out_proj(wsrc, bias_which, cast):
            marks = {0: load_wslice(wsrc, 0, 0, slice(0, 128), cast, q="gpsimd"),
                     1: load_wslice(wsrc, 1, 1, slice(0, 128), cast,
                                    q="gpsimd" if cast else "sync")}
            for o in range(NC):
                reg = PA if o % 2 == 0 else PB
                P.wait("tensor", marks[o][0], marks[o][1])
                for c in range(NC):
                    for j in range(4):
                        jsl = slice(j * 512, j * 512 + 512)
                        P.inc("tensor", mm(reg[:, jsl], WS[:, o % 2, c, 0:128], OHT[:, c, jsl],
                                           c == 0, c == NC - 1), "t")
                tmark = P.mark("t")
                if o + 2 < NC:
                    qn = "gpsimd" if (cast or o % 2 == 0) else "sync"
                    P.wait(qn, "t", tmark)
                    marks[o + 2] = load_wslice(wsrc, o + 2, o % 2, slice(0, 128), cast, q=qn)
                P.wait("vector", "t", tmark)
                P.inc("vector", lambda E, o=o, reg=reg:
                      E.scalar_tensor_tensor(XT[:, o, :], reg[:, :], BIAST[:, o, bias_which:bias_which + 1], XT[:, o, :],
                                             ALU.add, ALU.add), "v")
            P.barrier()

        def emit_attn_a(s):
            bmk = dma("gpsimd", SRB[32:33, 0:1024], biases[2:3, :], "dg")
            dma("gpsimd", WVB[:, :, :], wa['v'].rearrange("(c p) o -> p c o", p=128), "dg")
            wv = ("dg", P.mark("dg"))
            nm = emit_preln()
            if s == 0:
                dump(0, Y[:, :, :])
            emit_bvb(bmk)
            emit_vproj(wv, nm)
            if s == 0:
                dump(1, ARF)
            for p in range(8):
                mq = load_wslice(wa['q'], p, 0, slice(0, 128), True)
                mk = load_wslice(wa['k'], p, 0, slice(128, 256), True)
                emit_pair_qk(mq, mk, BIAST[:, p, 0:1], BIAST[:, p, 1:2])
                if s == 0 and p == 0:
                    dump(2, ARF)
                emit_perf_head(p, 0)
                if s == 0 and p == 0:
                    dump(3, ARF)
                emit_perf_head(p, 1)
                P.barrier()
            if s == 0:
                dump(4, ARF)
            emit_out_proj(wa['o'], 3, True)
            if s == 0:
                dump(5, XT[:, :, :])

        def emit_ffn(s, dbase=None, final=False):
            nm = emit_preln()
            if dbase is not None:
                dump(dbase, Y[:, :, :])
            for tc in range(4):
                tsl = slice(tc * 512, tc * 512 + 512)
                P.wait("tensor", "v", nm[tc])
                wf_marks = {0: ("ds", dma("sync", WFS[:, 0, :, :], swfcqT[0].rearrange("c p j -> p c j"), "ds"))}
                hb_done = {}
                for ht in range(NH):
                    bi = ht % 2
                    P.wait("tensor", wf_marks[ht][0], wf_marks[ht][1])
                    if ht >= 2:
                        P.wait("tensor", "s", hb_done[ht - 2])
                    for c in range(NC):
                        P.inc("tensor", mm(PB[:, bi * 512:bi * 512 + 512], WFS[:, bi, c, :], Y[:, c, tsl],
                                           c == 0, c == NC - 1), "t")
                    tmark = P.mark("t")
                    if ht + 1 < NH:
                        qn = "sync" if (ht + 1) % 2 == 0 else "gpsimd"
                        semn = "ds" if qn == "sync" else "dg"
                        if ht >= 1:
                            P.wait(qn, "t", tmark)
                        wf_marks[ht + 1] = (semn, dma(qn, WFS[:, (ht + 1) % 2, :, :],
                                                      swfcqT[ht + 1].rearrange("c p j -> p c j"), semn))
                    P.wait("scalar", "t", tmark)
                    hb_done[ht] = P.inc("scalar", act(HB[:, ht, :], PB[:, bi * 512:bi * 512 + 512], AF.Gelu), "s")
                    P.wait("vector", "s", hb_done[ht])
                    if ht >= 2:
                        P.wait("vector", "t", sqmm[ht - 2])
                    sv = P.inc("vector", lambda E, ht=ht, bi=bi:
                               E.tensor_mul(FSQ[:, bi, :], HB[:, ht, :], HB[:, ht, :]), "v")
                    P.wait("tensor", "s", hb_done[ht])
                    P.inc("tensor", mm(PB[0:1, 1024:1536], onesb[:, 0:1], HB[:, ht, :], ht == 0, ht == NH - 1), "t")
                    P.wait("tensor", "v", sv)
                    P.inc("tensor", mm(PB[0:1, 1536:2048], onesb[:, 0:1], FSQ[:, bi, :], ht == 0, ht == NH - 1), "t")
                    if ht == 0:
                        sqmm = {}
                    sqmm[ht] = P.mark("t")
                P.wait("scalar", "t")
                M_, Q_, T_, R_ = SR[0:1, 0:512], SR[0:1, 512:1024], SR[0:1, 1024:1536], SR[0:1, 1536:2048]
                P.inc("scalar", act(M_, PB[0:1, 1024:1536], AF.Copy, scale=1.0 / HID), "s")
                P.inc("scalar", act(Q_, PB[0:1, 1536:2048], AF.Copy, scale=1.0 / HID), "s")
                P.wait("vector", "s")
                P.inc("vector", lambda E: E.tensor_mul(T_, M_, M_), "v")
                vfence()
                P.inc("vector", lambda E: E.tensor_sub(T_, Q_, T_), "v")
                P.wait("scalar", "v")
                P.inc("scalar", act(R_, T_, AF.Ln, bias=CB[0:1, 0:1]), "s")
                sfence()
                P.inc("scalar", act(R_, R_, AF.Exp, scale=-0.5), "s")
                P.wait("tensor", "s")
                P.inc("tensor", mm(PA[:, 1536:2048], ONEF[0:1, :], R_, True, True), "t")
                P.wait("scalar", "t")
                rbm = P.inc("scalar", act(RBF[:, :], PA[:, 1536:2048], AF.Copy), "s")
                if dbase is not None and tc == 0:
                    dump(dbase + 1, ARF)
                    P.barrier()
                    dma("gpsimd", dbgf[:, 0:8], S2C[:, :], "st")
                    dma("gpsimd", dbg[dbase + 2][0:1, 0:1024], S1R[0:1, :], "st")
                    dma("gpsimd", dbg[dbase + 2][0:1, 1024:1536], SRB[0:1, 0:512], "st")
                    dma("gpsimd", dbg[dbase + 2][:, 2048:2560], RB[:, 0:512], "st")
                    P.barrier()
                wq_marks = {0: ("ds", dma("sync", WQS[:, 0, :, :], swpT[0].rearrange("(h p) e -> p h e", p=128), "ds"))}
                g_ev = {}
                for qd in range(4):
                    bi = qd % 2
                    P.wait("tensor", wq_marks[qd][0], wq_marks[qd][1])
                    P.wait("tensor", "s", rbm)
                    if qd >= 2:
                        P.wait("tensor", "v", g_ev[qd - 2])
                    for dc in range(2):
                        psl = slice(bi * 1024 + dc * 512, bi * 1024 + dc * 512 + 512)
                        for ht in range(NH):
                            P.inc("tensor", mm(PA[:, psl], WQS[:, bi, ht, dc * 128:dc * 128 + 128], HB[:, ht, :],
                                               ht == 0, False), "t")
                        P.inc("tensor", mm(PA[:, psl], S1R[0:1, qd * 256 + dc * 128:qd * 256 + dc * 128 + 128],
                                           M_, False, True), "t")
                    tmark = P.mark("t")
                    if qd + 1 < 4:
                        qn = "sync" if (qd + 1) % 2 == 0 else "gpsimd"
                        semn = "ds" if qn == "sync" else "dg"
                        if qd >= 1:
                            P.wait(qn, "t", tmark)
                        wq_marks[qd + 1] = (semn, dma(qn, WQS[:, (qd + 1) % 2, :, :],
                                                      swpT[qd + 1].rearrange("(h p) e -> p h e", p=128), semn))
                    P.wait("vector", "t", tmark)
                    for dc in range(2):
                        d = qd * 2 + dc
                        psl = slice(bi * 1024 + dc * 512, bi * 1024 + dc * 512 + 512)
                        FT1F = FFS[:, 0:512]
                        OF = FFS[:, 512:1024]
                        if not final:
                            vfence()
                            P.inc("vector", lambda E, psl=psl, FT1F=FT1F:
                                  E.tensor_mul(FT1F, PA[:, psl], RBF[:, :]), "v")
                            vfence()
                            g_ev[qd] = P.inc("vector", lambda E, d=d, tsl=tsl, FT1F=FT1F:
                                             E.scalar_tensor_tensor(XT[:, d, tsl], FT1F, S2C[:, d:d + 1],
                                                                    XT[:, d, tsl], ALU.add, ALU.add), "v")
                        else:
                            P.wait("vector", "t")
                            vfence()
                            P.inc("vector", lambda E, psl=psl, FT1F=FT1F:
                                  E.tensor_mul(FT1F, PA[:, psl], RBF[:, :]), "v")
                            vfence()
                            P.inc("vector", lambda E, d=d, tsl=tsl, FT1F=FT1F, OF=OF:
                                  E.scalar_tensor_tensor(OF, FT1F, S2C[:, d:d + 1], XT[:, d, tsl],
                                                         ALU.add, ALU.add), "v")
                            mvv = P.mark("v")
                            for sub in range(4):
                                tb = sub % 2
                                P.wait("tensor", "v", mvv)
                                if True:
                                    P.inc("tensor", tp(PB[:, tb * 512:tb * 512 + 128],
                                                       OF[:, sub * 128:sub * 128 + 128]), "t")
                                P.wait("scalar", "t")
                                smv = P.inc("scalar", act(SR[:, 1024 + tb * 128:1024 + tb * 128 + 128],
                                                          PB[:, tb * 512:tb * 512 + 128], AF.Copy), "s")
                                P.wait("sync", "s", smv)
                                dma("sync", out[s][tc * 512 + sub * 128:tc * 512 + sub * 128 + 128,
                                                   d * 128:d * 128 + 128],
                                    SR[:, 1024 + tb * 128:1024 + tb * 128 + 128], "st")
                                if sub + 2 <= 3:
                                    P.wait("tensor", "st")
                            g_ev[qd] = P.mark("v")
                            P.barrier(("vector",), sems=("st",))
                P.barrier()

        def emit_causal_head(p, hh):
            hs = hh * 64
            h = 2 * p + hh
            pieces = []
            for kc in range(NT):
                qlo = kc * 128
                for j in range(qlo // 512, 4):
                    pieces.append((kc, j, max(qlo, j * 512), j * 512 + 512))
            ev_marks = {}
            av_marks = {}
            for pi, (kc, j, a, b) in enumerate(pieces):
                bi = pi % 2
                ssl = slice(bi * 1024, bi * 1024 + (b - a))
                if pi >= 2:
                    P.wait("tensor", "s", ev_marks[pi - 2])
                P.inc("tensor", mm(PB[:, ssl], KP[hs:hs + 64, kc * 128:kc * 128 + 128],
                                   QP[hs:hs + 64, a:b], True, True), "t")
                tm = P.mark("t")
                P.wait("scalar", "t", tm)
                if pi >= 2:
                    P.wait("scalar", "t", av_marks[pi - 2])
                em = P.inc("scalar", act(PBUF[:, bi, 0:b - a], PB[:, ssl], AF.Exp,
                                         bias=NEGB[:, hh:hh + 1], scale=0.125), "s")
                ev_marks[pi] = em
                if a == kc * 128:
                    P.wait("vector", "s", em)
                    em2 = P.inc("vector", lambda E, bi=bi:
                                E.tensor_mul(PBUF[:, bi, 0:128], PBUF[:, bi, 0:128], triub[:, :]), "v")
                    P.wait("tensor", "v", em2)
                else:
                    P.wait("tensor", "s", em)
                P.inc("tensor", mm(PA[0:65, a:b], VAUG[:, kc, h * 65:h * 65 + 65],
                                   PBUF[:, bi, 0:b - a], kc == 0, kc == min(4 * j + 3, NT - 1)), "t")
                av_marks[pi] = P.mark("t")
            P.wait("scalar", "t")
            lm = P.inc("scalar", act(SR[0:1, :], PA[64:65, :], AF.Ln), "s")
            P.wait("scalar", "s", lm)
            cm = P.inc("scalar", act(SRB[0:1, :], SR[0:1, :], AF.Exp, scale=-1.0), "s")
            sm = P.inc("scalar", act(NUM[0:64, :], PA[0:64, :], AF.Copy), "s")
            P.wait("tensor", "s", cm)
            for j in range(4):
                jsl = slice(j * 512, j * 512 + 512)
                P.inc("tensor", mm(PB[:, jsl], onesb[0:1, :], SRB[0:1, jsl], True, True), "t")
            P.wait("vector", "t")
            P.wait("vector", "s", sm)
            P.inc("vector", lambda E: E.tensor_mul(OHT[hs:hs + 64, p, :], NUM[0:64, :], PB[0:64, :]), "v")
            P.barrier(("tensor", "scalar"), sems=("v",))

        def emit_attn_b(s):
            bmk = dma("gpsimd", SRB[32:33, 0:1024], biases[6:7, :], "dg")
            dma("sync", WVB[:, :, :], swbT[2].rearrange("(c p) o -> p c o", p=128), "ds")
            wv = ("ds", P.mark("ds"))
            nm = emit_preln()
            emit_bvb(bmk)
            emit_vproj(wv, nm)
            for p in range(8):
                mq = load_wslice(swbT[0], p, 0, slice(0, 128), False, q="sync")
                mk = load_wslice(swbT[1], p, 0, slice(128, 256), False, q="gpsimd")
                emit_pair_qk(mq, mk, BIAST[:, p, 4:5], BIAST[:, p, 5:6])
                vv = P.inc("vector", lambda E: E.tensor_mul(QAK[:, :], QP[:, :], QP[:, :]), "v")
                P.wait("tensor", "v", vv)
                for hh in range(2):
                    hsl = slice(hh * 64, hh * 64 + 64)
                    reg = PA if hh == 0 else PB
                    for j in range(4):
                        jsl = slice(j * 512, j * 512 + 512)
                        P.inc("tensor", mm(reg[0:1, jsl], onesb[hsl, 0:1], QAK[hsl, jsl], True, True), "t")
                P.wait("vector", "t")
                P.inc("vector", lambda E: E.tensor_reduce(SR[0:1, 0:1], PA[0:1, :], AX.X, ALU.max), "v")
                vv = P.inc("vector", lambda E: E.tensor_reduce(SR[0:1, 1:2], PB[0:1, :], AX.X, ALU.max), "v")
                vv = P.inc("vector", lambda E: E.tensor_mul(QAK[:, :], KP[:, :], KP[:, :]), "v")
                P.wait("tensor", "v", vv)
                for hh in range(2):
                    hsl = slice(hh * 64, hh * 64 + 64)
                    reg = PA if hh == 0 else PB
                    for j in range(4):
                        jsl = slice(j * 512, j * 512 + 512)
                        P.inc("tensor", mm(reg[0:1, jsl], onesb[hsl, 0:1], QAK[hsl, jsl], True, True), "t")
                P.wait("vector", "t")
                P.inc("vector", lambda E: E.tensor_reduce(SR[0:1, 2:3], PA[0:1, :], AX.X, ALU.max), "v")
                P.inc("vector", lambda E: E.tensor_reduce(SR[0:1, 3:4], PB[0:1, :], AX.X, ALU.max), "v")
                vfence()
                P.inc("vector", lambda E: E.tensor_add(SR[0:1, 4:5], SR[0:1, 0:1], SR[0:1, 2:3]), "v")
                vfence()
                P.inc("vector", lambda E: E.tensor_add(SR[0:1, 5:6], SR[0:1, 1:2], SR[0:1, 3:4]), "v")
                P.wait("scalar", "v")
                P.inc("scalar", act(SRB[0:1, 0:2], SR[0:1, 4:6], AF.Copy, scale=1.0 / 16.0), "s")
                P.wait("tensor", "s")
                P.inc("tensor", mm(PA[:, 0:1], onesb[0:1, :], SRB[0:1, 0:1], True, True), "t")
                P.inc("tensor", mm(PA[:, 1:2], onesb[0:1, :], SRB[0:1, 1:2], True, True), "t")
                P.wait("scalar", "t")
                P.inc("scalar", act(NEGB[:, 0:2], PA[:, 0:2], AF.Copy, scale=-1.0), "s")
                P.barrier(("tensor", "vector", "scalar"), sems=("s", "t", "v"))
                emit_causal_head(p, 0)
                emit_causal_head(p, 1)
                P.barrier()
            emit_out_proj(swbT[3], 7, False)

        def emit_exit(s):
            YX = Y[:, :, :].rearrange("p a b -> p (a b)").rearrange("p (tt c j) -> p tt c j", tt=NT, c=NC)
            for c in range(NC):
                dmat(YX[:, :, c, :], XT[:, c, :])
            P.wait("gpsimd", "ds")
            dma("gpsimd", out[s].rearrange("(tt p) (c j) -> p tt c j", p=128, j=128), YX[:, :, :, :], "st")
            P.barrier()

        for s in range(nseq):
            emit_entry(s)
            if stop == "entry":
                emit_exit(s)
                continue
            emit_attn_a(s)
            if stop == "attn_a":
                emit_exit(s)
                continue
            emit_ffn(s, dbase=6 if (dbgmode and s == 0) else None)
            if stop == "ffn1":
                emit_exit(s)
                continue
            emit_attn_b(s)
            if stop == "attn_b":
                emit_exit(s)
                continue
            emit_ffn(s, final=True)

        P.wait("gpsimd", "st")
        P.wait("sync", "st")

        with nc.Block() as block:
            prog.replay(block)
    return nc


def make_inputs(inputs):
    q = np.ascontiguousarray(inputs["L0_attn_a_queries"], dtype=np.float32)
    bias = np.stack([
        inputs["L0_attn_a_bq"], inputs["L0_attn_a_bk"], inputs["L0_attn_a_bv"], inputs["L0_attn_a_bo"],
        inputs["L0_attn_b_bq"], inputs["L0_attn_b_bk"], inputs["L0_attn_b_bv"], inputs["L0_attn_b_bo"],
    ]).astype(np.float32)
    lnwb = np.stack([inputs["L0_ffn_ln_w"], inputs["L0_ffn_ln_b"]]).astype(np.float32)
    shared = {
        "waq": np.ascontiguousarray(inputs["L0_attn_a_Wq"], dtype=np.float32),
        "wak": np.ascontiguousarray(inputs["L0_attn_a_Wk"], dtype=np.float32),
        "wav": np.ascontiguousarray(inputs["L0_attn_a_Wv"], dtype=np.float32),
        "wao": np.ascontiguousarray(inputs["L0_attn_a_Wo"], dtype=np.float32),
        "aproj": np.ascontiguousarray(inputs["L0_attn_a_proj"], dtype=np.float32),
        "wbq": np.ascontiguousarray(inputs["L0_attn_b_Wq"], dtype=np.float32),
        "wbk": np.ascontiguousarray(inputs["L0_attn_b_Wk"], dtype=np.float32),
        "wbv": np.ascontiguousarray(inputs["L0_attn_b_Wv"], dtype=np.float32),
        "wbo": np.ascontiguousarray(inputs["L0_attn_b_Wo"], dtype=np.float32),
        "biases": bias,
        "wfc": np.ascontiguousarray(inputs["L0_ffn_W_fc"], dtype=np.float32),
        "wproj": np.ascontiguousarray(inputs["L0_ffn_W_proj"], dtype=np.float32),
        "lnwb": lnwb,
        "eye_f": np.eye(128, dtype=np.float32),
        "ones_b": np.ones((128, 128), dtype=ml_dtypes.bfloat16),
        "ones_f": np.ones((1, 128), dtype=np.float32),
        "triu_b": np.triu(np.ones((128, 128))).astype(ml_dtypes.bfloat16),
    }
    in_maps = []
    for c in range(NCORE):
        m = dict(shared)
        m["xq"] = np.ascontiguousarray(q[c * NSEQ:(c + 1) * NSEQ])
        in_maps.append(m)
    return in_maps


def kernel(**inputs):
    nc = build_program()
    in_maps = make_inputs(inputs)
    res = run_bass_kernel_spmd(nc, in_maps, core_ids=list(range(NCORE)))
    return np.concatenate([np.asarray(r["out"], dtype=np.float32) for r in res.results], axis=0)
